# version 23
# baseline (speedup 1.0000x reference)
import math
import contextlib
import numpy as np
import ml_dtypes
import concourse.bass as bass
import concourse.mybir as mybir

F32 = mybir.dt.float32
BF16 = mybir.dt.bfloat16
U32 = mybir.dt.uint32
AF = mybir.ActivationFunctionType
ALU = mybir.AluOpType
AX = mybir.AxisListType

D = 2048
FF = 8192
NE, NR = 8, 8
ER = NE * NR
EPS = 1e-5
NCH = 16
NFC = 64

TWO_PI = 2.0 * math.pi
CW1 = 6.28125
CW2 = float(np.float32(TWO_PI - CW1))
CW3 = float(np.float64(TWO_PI) - CW1 - np.float64(np.float32(TWO_PI - CW1)))

HEAD_ORDER = [(a + 16 * h) for a in range(16) for h in range(2)]


class Prog:
    def __init__(self, nc, sem_names):
        self.nc = nc
        self.q = {e: [] for e in ("sync", "tensor", "vector", "scalar", "gpsimd")}
        self.kinds = {e: [] for e in self.q}
        self.cnt = {s: 0 for s in sem_names}
        self.sem_names = list(sem_names)

    def _emit_wait(self, eng, s, v):
        self.kinds[eng].append("w")
        self.q[eng].append(lambda e, sems, s=s, v=v: e.wait_ge(sems[s], v))

    def wait(self, eng, tok):
        if tok is not None:
            self._emit_wait(eng, tok[0], tok[1])

    def waitl(self, eng, toks):
        best = {}
        for t in toks:
            if t is None:
                continue
            s, v = t
            if s not in best or v > best[s]:
                best[s] = v
        for s, v in best.items():
            self._emit_wait(eng, s, v)

    def now(self, sem):
        return (sem, self.cnt[sem]) if self.cnt[sem] > 0 else None

    def _op(self, eng, fn, tick, tickn=1):
        self.kinds[eng].append("d" if tickn == 16 else "o")
        self.cnt[tick] += tickn
        tok = (tick, self.cnt[tick])

        def run(e, sems, f=fn, s=tick, n=tickn):
            f(e).then_inc(sems[s], n)
        self.q[eng].append(run)
        return tok

    def top(self, fn):
        return self._op("tensor", fn, "T")

    def vop(self, fn):
        if self.cnt["V"] > 0:
            self._emit_wait("vector", "V", self.cnt["V"])
        return self._op("vector", fn, "V")

    def sop(self, fn):
        if self.cnt["S"] > 0:
            self._emit_wait("scalar", "S", self.cnt["S"])
        return self._op("scalar", fn, "S")

    def gop(self, fn):
        if self.cnt["G"] > 0:
            self._emit_wait("gpsimd", "G", self.cnt["G"])
        return self._op("gpsimd", fn, "G")

    def dma(self, eng, dst, src, tick):
        def fn(e, d=dst, s=src):
            n0 = self.nc.n_instructions()
            ins = e.dma_start(d, s)
            n1 = self.nc.n_instructions()
            if n1 - n0 != 1:
                import traceback
                print(f"DMA SPLIT x{n1 - n0}: dst={d.tensor.name if hasattr(d, 'tensor') else d}")
                for l in traceback.format_stack()[-8:-4]:
                    print("   ", l.split(chr(10))[0])
            return ins
        return self._op(eng, fn, tick, 16)

    def join_all(self):
        toks = [self.now(s) for s in self.sem_names]
        for e in ("tensor", "vector", "scalar", "gpsimd", "sync"):
            self.waitl(e, toks)

    def play(self):
        nc = self.nc
        with contextlib.ExitStack() as stx:
            sems = {n: stx.enter_context(nc.semaphore(n)) for n in self.sem_names}
            block = stx.enter_context(nc.Block())
            for eng_name in ("sync", "tensor", "vector", "scalar", "gpsimd"):
                fns = self.q[eng_name]

                def body(e, fns=fns, sems=sems):
                    for f in fns:
                        f(e, sems)
                getattr(block, eng_name)(body)


def bc16(ap):
    return ap.bitcast(BF16)


def build(N, T):
    assert T % 64 == 0 and N % T == 0 and T <= 512
    NT = N // T
    NG = T // 64
    NHALF = 2 if NG % 2 == 0 else 1
    GPH = NG // NHALF
    UGH = 4 * GPH
    nc = bass.Bass("TRN2", target_bir_lowering=False, debug=False)

    inp = lambda name, shape, dtype=F32: nc.dram_tensor(
        name, list(shape), dtype, kind="ExternalInput").ap()
    sdram = lambda name, shape, dtype: nc.dram_tensor(
        name, list(shape), dtype, kind="Internal").ap()

    xT = inp("xT", [D, N])
    wq = inp("wq", [4, 16, 128, 512]); wk = inp("wk", [4, 16, 128, 512])
    wv = inp("wv", [4, 16, 128, 512]); wo = inp("wo", [4, 16, 128, 512])
    w1 = inp("w1", [16, 16, 128, 512])
    w2 = inp("w2", [8, 64, 128, 256])
    a1t = inp("a1t", [16, 128, 64])
    b1rf = inp("b1rf", [ER, FF])
    a2t = inp("a2t", [64, 128, 64])
    b2rf = inp("b2rf", [ER, D])
    wrt = inp("wrt", [16, 128, 8])
    b1c = inp("b1c", [128, NFC])
    b2c4_d = inp("b2c4d", [1, 2048])
    brrow_d = inp("brrow", [NE, 1])
    gam_d = inp("gam", [128, NCH]); bet_d = inp("bet", [128, NCH])
    identb_d = inp("identb", [128, 128], BF16)
    identf_d = inp("identf", [128, 128])
    pmat_d = inp("pmat", [128, 128], BF16)
    ecs_d = inp("ecs", [32, 2048], BF16)
    onesm_d = inp("onesmd", [128, 128])
    onesr_d = inp("onesred", [128, 1])
    eiota_d = inp("eiota", [64, 1])
    invf_d = inp("invf32", [32, 1])
    toff_d = inp("toff", [32, 1])
    eps_d = inp("epscol", [128, 1])
    iota_d = inp("iota32d", [32, 512])

    out = nc.dram_tensor("out", [D, N], F32, kind="ExternalOutput").ap()

    xp = sdram("xp", [D, N], F32)
    DBG = False
    PHASE = 99
    snaps = []
    if DBG:
        TT_ = T
        dbg_p = nc.dram_tensor("dbg_p", [NT, T // 64, 128, 512], BF16,
                               kind="ExternalOutput").ap()
        dbg_q = nc.dram_tensor("dbg_q", [NT, 128, 16, TT_], BF16, kind="ExternalOutput").ap()
        dbg_k = nc.dram_tensor("dbg_k", [NT, 128, 16, TT_], BF16, kind="ExternalOutput").ap()
        dbg_v = nc.dram_tensor("dbg_v", [NT, 64, 32, TT_], BF16, kind="ExternalOutput").ap()
        dbg_fm = nc.dram_tensor("dbg_fm", [NT, 128, 16, TT_], BF16, kind="ExternalOutput").ap()
    wqb = sdram("wqb", [4, 16, 128, 512], BF16)
    wkb = sdram("wkb", [4, 16, 128, 512], BF16)
    wvb = sdram("wvb", [4, 16, 128, 512], BF16)
    wob = sdram("wob", [4, 16, 128, 512], BF16)
    w1b = sdram("w1b", [16, 16, 128, 512], BF16)
    w2b = sdram("w2b", [8, 64, 128, 256], BF16)
    b1rfb = sdram("b1rfb", [ER, FF], BF16)

    p = Prog(nc, ("T", "V", "S", "G", "DW", "DX", "DG", "DW2", "DW3", "DXP",
                  "DX0", "DX1", "DO0", "DO1", "DB0", "DB1", "PA0", "PA1",
                  "PA2", "PA3", "PB0", "PB1", "PB2", "PB3"))

    stx = contextlib.ExitStack()
    sb = lambda name, shape, dtype=F32: stx.enter_context(
        nc.sbuf_tensor(name, list(shape), dtype))
    psm = lambda name, shape, dtype=F32: stx.enter_context(
        nc.psum_tensor(name, list(shape), dtype))

    arH = sb("arH", [128, 16384])
    arC = sb("arC", [128, 8192])
    arB = sb("arB", [128, 8192])
    arE = sb("arE", [128, 4096])
    arD = sb("arD", [128, 4096])
    arX = sb("arX", [128, 2048])

    a1sb = sb("a1sb", [128, 16 * 64], BF16)
    a2sb = sb("a2sb", [128, 64 * 64], BF16)
    b2sb = sb("b2sb", [ER + 1, D], BF16)
    wrsb = sb("wrsb", [128, 16 * 8], BF16)
    gamsb = sb("gamsb", [128, NCH]); betsb = sb("betsb", [128, NCH])
    b1csb = sb("b1csb", [128, NFC])
    brsb = sb("brsb", [40, 1])
    idbsb = sb("idbsb", [128, 128], BF16); idfsb = sb("idfsb", [128, 128])
    pmsb = sb("pmsb", [128, 128], BF16)
    ecssb = sb("ecssb", [32, 2048], BF16)
    onesm = sb("onesm", [128, 128])
    onrsb = sb("onrsb", [128, 1])
    eisb = sb("eisb", [64, 1]); ivsb = sb("ivsb", [32, 1]); tosb = sb("tosb", [32, 1])
    epsc = sb("epsc", [128, 1])
    cossb = sb("cossb", [32, T], BF16); sinsb = sb("sinsb", [32, T], BF16)
    rowm2 = sb("rowm2", [128, 2048])
    NSUB = max(T // 128, 1)
    TS_ = min(T, 128)
    lgtm = sb("lgtm", [128, 8 * NSUB])
    mets = sb("mets", [128, 4 * NSUB])
    srt8 = sb("srt8", [128, 8])
    idx8 = sb("idx8", [128, 8], U32)
    idxf = sb("idxf", [128, 8])
    esc8 = sb("esc8", [128, 8])
    negm = sb("negm", [128, 4])
    smax = sb("smax", [128, 16]); ssum = sb("ssum", [128, 16]); srec = sb("srec", [128, 16])

    accA = psm("accA", [128, 512])
    accB = psm("accB", [128, 512])
    bcA = psm("bcA", [128, 512])
    bcB = psm("bcB", [128, 512])
    pscr = psm("pscr", [128, 512])
    bfA = psm("bfA", [128, 1024], BF16)
    bfB = psm("bfB", [128, 1024], BF16)
    u2ps = psm("u2ps", [128, 512])

    k_buf = bc16(arH[:, 0:4096])
    sbufT = bc16(arH[:, 4096:4096 + 128 * UGH])
    hw = bc16(arH[:, 0:(NFC * T) // 2])
    z_buf = bc16(arC[:, 0:4096])
    attnfm = bc16(arC[:, 0:4096])
    q_buf = bc16(arC[:, 4096:8192])
    xn2 = bc16(arC[:, 0:4096])
    w1buf = bc16(arC[:, 4096:8192])
    xcb = arB
    v_alt = bc16(arB[0:64, :])
    w2buf = bc16(arB)
    wbuf = bc16(arE[:, 0:4096])
    S_grp = bc16(arD[:, 0:1024])
    M_grp = bc16(arD[:, 1024:1536])
    P_grp = bc16(arD[:, 1536:1792])
    P4sb = bc16(arD[:, 1792:2048])
    V4sb = bc16(arD[:, 2048:2176])
    pv_sb = bc16(arD[:, 2176:2304])
    sc_sc = arD[:, 2304:2816]
    pq_sc = bc16(arD[:, 2816:3072])
    cexp = bc16(arD[:, 3072:3328])
    sexp = bc16(arD[:, 3328:3584])
    base1 = bc16(arD[:, 0:256])
    h0sb = bc16(arD[:, 256:512])
    h1sb = bc16(arD[:, 512:768])
    wb0 = bc16(arD[:, 768:1024])
    wb1 = bc16(arD[:, 1024:1280])
    ohm0 = bc16(arD[0:64, 1280:1536])
    ohm1 = bc16(arD[0:64, 1536:1792])
    z1k0 = bc16(arD[0:64, 1792:2048])
    z1k1 = bc16(arD[0:64, 2048:2304])
    z2wsb = bc16(arD[0:64, 2304:2560])
    wsrow64 = bc16(arD[64:65, 2304:2560])
    u1sb = arD[0:64, 2560:3072]
    b1f0 = bc16(arD[0:64, 3072:3328])
    b1f1 = bc16(arD[0:64, 3328:3584])
    sqs = arH[:, 0:1024]
    tsc = arD[:, 3584:4096]
    tscB = arD[:, 3072:3584]
    whalf = [bc16(arE[:, 0:2048]), bc16(arE[:, 2048:4096])]
    zsc0 = arD[:, 3584:4096][0:64, :]
    tgr = rowm2[0:32, 0:512][:, 0:T]
    angr = rowm2[0:32, 512:1024][:, 0:T]
    an2r = tgr
    nrr = rowm2[0:32, 1024:1536][:, 0:T]
    xch0 = arX[:, 1024:1536]
    xch1 = arX[:, 1536:2048]
    lgsb = arX[32:32 + NE, 0:512]

    STG = lambda i: arX[:, 512 * (i % 2):512 * (i % 2) + T]

    R_M1, R_R1, R_RR, R_MM, R_T1, R_T2, R_T3, R_WS, R_I0, R_I1, R_W0, R_W1 = range(12)
    ROWMAP = {R_R1: (0, 0), R_T1: (0, 1), R_T2: (0, 2), R_T3: (0, 3),
              R_M1: (32, 0), R_W0: (32, 1), R_W1: (32, 2), R_WS: (32, 3),
              R_RR: (64, 0), R_MM: (64, 1), R_I0: (64, 2), R_I1: (64, 3)}

    def ROWS(i):
        b, cb = ROWMAP[i]
        return rowm2[b:b + 1, 512 * cb:512 * cb + T]

    RBASE = lambda i: ROWMAP[i][0]

    XCC = lambda c: xcb[:, 512 * c:512 * c + T]
    ZCH = lambda c: z_buf[:, 512 * c:512 * c + T]
    XN = lambda c: xn2[:, 512 * c:512 * c + T]

    def rank1(dst_ps, row_i, nrows=128):
        b = RBASE(row_i)
        return p.top(lambda t, b=b, row_i=row_i, nrows=nrows: t.matmul(
            dst_ps, onesm[b:b + 1, 0:nrows], ROWS(row_i), start=True, stop=True,
            tile_position=(b, 0)))

    SNAPN = 1
    snap_hits = {"n": 0}
    marks = []
    nc._marks = marks

    def mark(name):
        marks.append((name, {e: len(p.q[e]) for e in p.q}))

    def snap(k):
        mark(str(k))
        if PHASE == k and not snaps:
            snap_hits["n"] += 1
            if snap_hits["n"] >= SNAPN:
                snaps.append(({e: len(p.q[e]) for e in p.q}, dict(p.cnt)))

    stg_f32 = [arH[:, 4096 * i:4096 * i + 4096] for i in range(4)]
    cast_bf = [bc16(arC[:, 2048 * i:2048 * i + 2048]) for i in range(4)]
    PA = ("PA0", "PA1", "PA2", "PA3")
    PB = ("PB0", "PB1", "PB2", "PB3")

    cast_state = {"i": 0, "pb": [None] * 4, "ce": [None] * 4}

    def cast_chunk(src_f32, dst_bf16_dram, ncols=4096):
        i = cast_state["i"]; cast_state["i"] += 1
        slot = i % 4
        p.waitl("sync", [cast_state["ce"][slot]])
        tk_in = p.dma("sync", stg_f32[slot][:, 0:ncols], src_f32, tick=PA[slot])
        p.wait("vector", tk_in)
        p.waitl("vector", [cast_state["pb"][slot]])
        tk_c = p.vop(lambda v, slot=slot, ncols=ncols: v.tensor_copy(
            cast_bf[slot][:, 0:ncols], stg_f32[slot][:, 0:ncols]))
        cast_state["ce"][slot] = tk_c
        p.wait("scalar", tk_c)
        tk_out = p.dma("scalar", dst_bf16_dram, cast_bf[slot][:, 0:ncols],
                       tick=PB[slot])
        cast_state["pb"][slot] = tk_out
        return tk_out

    def cast_small(src3d, dst2d, nrows, nc_, nf_):
        i = cast_state["i"]; cast_state["i"] += 1
        slot = i % 4
        ncols = nc_ * nf_
        p.waitl("sync", [cast_state["ce"][slot]])
        stv = stg_f32[slot][0:nrows, 0:ncols].rearrange("p (c f) -> p c f", f=nf_)
        tk_in = p.dma("sync", stv, src3d, tick=PA[slot])
        p.wait("vector", tk_in)
        tk_c = p.vop(lambda v, slot=slot, nrows=nrows, ncols=ncols, dst2d=dst2d:
                     v.tensor_copy(dst2d, stg_f32[slot][0:nrows, 0:ncols]))
        cast_state["ce"][slot] = tk_c
        return tk_c

    cast_small(a1t.rearrange("c p f -> p c f"), a1sb[:], 128, 16, 64)
    cast_small(a2t.rearrange("c p f -> p c f"), a2sb[:], 128, 64, 64)
    cast_small(b2rf[:].rearrange("e (c f) -> e c f", c=1), b2sb[0:64, :], 64, 1, 2048)
    cast_small(wrt.rearrange("c p f -> p c f"), wrsb[:], 128, 16, 8)
    cast_small(b2c4_d.rearrange("e (c f) -> e c f", c=1), b2sb[64:65, :], 1, 1, 2048)
    tok_casts = ("V", p.cnt["V"])

    for dstt, srct in [(gamsb[:], gam_d[:]), (betsb[:], bet_d[:]), (b1csb[:], b1c[:]),
                       (brsb[32:32 + NE, 0:1], brrow_d[:]),
                       (idfsb[:], identf_d[:]), (onesm[:], onesm_d[:]),
                       (onrsb[:], onesr_d[:]), (eisb[:], eiota_d[:]),
                       (ivsb[:], invf_d[:]), (tosb[:], toff_d[:]),
                       (epsc[:], eps_d[:]), (idbsb[:], identb_d[:]),
                       (pmsb[:], pmat_d[:]), (ecssb[:], ecs_d[:])]:
        tok_loads = p.dma("scalar", dstt, srct, tick="DXP")
    for e in ("tensor", "vector", "scalar", "gpsimd", "sync"):
        p.wait(e, tok_loads)
        p.wait(e, tok_casts)
        for sn in PA + PB:
            p.wait(e, p.now(sn))
        p.wait(e, p.now("S"))
        p.wait(e, p.now("G"))
    snap(1)

    FSTG = [arH[:, 8192:10240], arH[:, 11264:13312]]
    FDST = [bc16(arH[:, 10240:11264]), bc16(arH[:, 13312:14336])]
    feed = {"i": 0, "ce": [None, None], "pb": [None, None]}

    def fpipe(src3, scratch3, B, F, dst_bf=None, pre_cast=()):
        slot = feed["i"] % 2; feed["i"] += 1
        p.waitl("sync", [feed["ce"][slot]])
        stv = FSTG[slot].rearrange("p (c f) -> p c f", f=F)[:, 0:B, :]
        tk_in = p.dma("sync", stv, src3, tick=PA[slot])
        eng = "vector" if (dst_bf is not None or feed["i"] % 2 == 0) else "scalar"
        p.wait(eng, tk_in)
        p.waitl(eng, [feed["pb"][slot]] + list(pre_cast))
        if dst_bf is None:
            if feed["i"] % 2 == 0:
                tk_c = p.vop(lambda v, slot=slot: v.tensor_copy(
                    FDST[slot][:], FSTG[slot][:]))
            else:
                tk_c = p.sop(lambda sx, slot=slot: sx.copy(
                    FDST[slot][:], FSTG[slot][:]))
            osrc = FDST[slot].rearrange("p (c f) -> p c f", f=F)[:, 0:B, :]
        else:
            tk_c = p.vop(lambda v, slot=slot, dst_bf=dst_bf, stv=stv:
                         v.tensor_copy(dst_bf, stv))
            osrc = dst_bf
        feed["ce"][slot] = tk_c
        p.wait("scalar", tk_c)
        tk_out = p.dma("scalar", scratch3, osrc, tick=PB[slot])
        feed["pb"][slot] = tk_out
        return tk_c, tk_out

    feed_pend = []
    feed_done = {}

    def feeder_fill():
        for fC in range(16):
            for j in range(4):
                src = w1[fC].rearrange("b p f -> p b f")[:, 4 * j:4 * j + 4, :]
                scr = w1b[fC].rearrange("b p f -> p b f")[:, 4 * j:4 * j + 4, :]
                feed_pend.append(("w1", src, scr, 4, 512))
        b1s = b1rf.rearrange("e (h q) -> (e h) q", h=2)
        b1d = b1rfb.rearrange("e (h q) -> (e h) q", h=2)
        for j in range(2):
            feed_pend.append(("b1", b1s[:, 2048 * j:2048 * j + 2048].rearrange(
                "p (c f) -> p c f", f=2048),
                b1d[:, 2048 * j:2048 * j + 2048].rearrange(
                "p (c f) -> p c f", f=2048), 1, 2048))
        for dC in range(8):
            for j in range(8):
                src = w2[dC].rearrange("b p f -> p b f")[:, 8 * j:8 * j + 8, :]
                scr = w2b[dC].rearrange("b p f -> p b f")[:, 8 * j:8 * j + 8, :]
                feed_pend.append(("w2", src, scr, 8, 256))

    def pump(n):
        for _ in range(n):
            if not feed_pend:
                return
            cls, src, scr, B, F = feed_pend.pop(0)
            fpipe(src, scr, B, F)
            if not feed_pend or feed_pend[0][0] != cls:
                feed_done[cls] = [feed["pb"][0], feed["pb"][1]]

    def pump_class(cls):
        while feed_pend and feed_pend[0][0] == cls:
            pump(1)

    def qkvo_fetch(wscr, wf32, h, pre):
        fC, pair = h // 2, h % 2
        tc = None
        for cq in range(2):
            c0 = 256 * pair + 128 * cq
            src = wf32[fC].rearrange("c p f -> p c f")[:, :, c0:c0 + 128]
            scr = wscr[fC].rearrange("c p f -> p c f")[:, :, c0:c0 + 128]
            dstv = whalf[h % 2][:].rearrange("p (c f) -> p c f", f=256)[
                :, :, 128 * cq:128 * cq + 128]
            tc, _ = fpipe(src, scr, 16, 128, dst_bf=dstv, pre_cast=pre)
        return tc

    prev_r3b = None
    for it in range(NT):
        t0 = it * T
        xsl = xT.rearrange("(c p) t -> p c t", p=128)[:, :, t0:t0 + T]
        xpsl = xp.rearrange("(c p) t -> p c t", p=128)[:, :, t0:t0 + T]
        outsl = out.rearrange("(c p) t -> p c t", p=128)[:, :, t0:t0 + T]

        p.waitl("scalar", [prev_r3b])
        tk_io = p.dma("scalar", tgr, iota_d[:, 0:T], tick="DXP")
        p.waitl("vector", [tk_io, prev_r3b])
        p.vop(lambda v, t0=t0: v.tensor_scalar(tgr, tgr, tosb[:], float(t0),
                                               ALU.add, ALU.add))
        p.vop(lambda v: v.tensor_scalar(angr, tgr, ivsb[:], None, ALU.mult))

        nintv = u2ps.bitcast(mybir.dt.int32)[0:32, 0:T]
        dscr = pscr[0:32, 0:T]

        def range_reduce(src, dst):
            p.vop(lambda v: v.tensor_scalar(nrr, src, 1.0 / TWO_PI, None, ALU.mult))
            p.vop(lambda v: v.tensor_copy(nintv, nrr))
            p.vop(lambda v: v.tensor_copy(dst, nintv))
            p.vop(lambda v: v.tensor_sub(dscr, nrr, dst))
            p.vop(lambda v: v.tensor_scalar(nrr, dscr, 0.5, None, ALU.is_gt))
            p.vop(lambda v: v.tensor_add(dst, dst, nrr))
            p.vop(lambda v: v.tensor_scalar(nrr, dscr, -0.5, None, ALU.is_lt))
            p.vop(lambda v: v.tensor_sub(dst, dst, nrr))
            p.vop(lambda v: v.scalar_tensor_tensor(nrr, dst, -CW1, src,
                                                   ALU.mult, ALU.add))
            p.vop(lambda v: v.scalar_tensor_tensor(nrr, dst, -CW2, nrr,
                                                   ALU.mult, ALU.add))
            p.vop(lambda v: v.scalar_tensor_tensor(dst, dst, -CW3, nrr,
                                                   ALU.mult, ALU.add))
            p.vop(lambda v: v.tensor_scalar(dst, dst, math.pi, None, ALU.min))
            return p.vop(lambda v: v.tensor_scalar(dst, dst, -math.pi, None, ALU.max))

        tk = range_reduce(angr, tgr)
        p.wait("scalar", tk)
        tk_sin = p.sop(lambda s: s.activation(sinsb[:], tgr, AF.Sin))
        p.wait("vector", tk_sin)
        p.vop(lambda v: v.tensor_scalar(an2r, angr, math.pi / 2.0, None, ALU.add))
        tk = range_reduce(an2r, angr)
        p.wait("scalar", tk)
        tk_cos = p.sop(lambda s: s.activation(cossb[:], angr, AF.Sin))
        snap(2)

        p.join_all()
        p.vop(lambda v: v.memset(S_grp[:], 0.0))
        p.vop(lambda v: v.memset(P4sb[:], 0.0))
        q_pre = []
        if it == 0:
            feeder_fill()
            q_pre.append(qkvo_fetch(wqb, wq, 0, []))
            q_pre.append(qkvo_fetch(wqb, wq, 1, []))
        else:
            for h_ in range(2):
                srcw = wqb[0].rearrange("c p f -> p c f")[
                    :, :, 256 * h_:256 * h_ + 256]
                dstw = whalf[h_][:].rearrange("p (c f) -> p c f", f=256)
                q_pre.append(p.dma("sync", dstw, srcw,
                                   tick="DW" if h_ == 0 else "DG"))

        tk_dma = {}
        xbuf = lambda c: (xch0 if c % 2 == 0 else xch1)[:, 0:T]
        for c in range(min(2, NCH)):
            tk_dma[c] = p.dma("scalar", XCC(c), xsl[:, c, :], tick="DX0" if c % 2 == 0 else "DX1")
        for c in range(NCH):
            p.wait("tensor", tk_dma[c])
            tk_m1 = p.top(lambda t, c=c: t.matmul(
                pscr[0:1, :T], onrsb[:], XCC(c), start=(c == 0), stop=(c == NCH - 1)))
            if c + 2 < NCH:
                p.wait("scalar", tk_m1)
                tk_dma[c + 2] = p.dma("scalar", XCC(c + 2), xsl[:, c + 2, :],
                                      tick="DX0" if c % 2 == 0 else "DX1")
        p.wait("vector", tk_m1)
        tk_m1r = p.vop(lambda v: v.tensor_scalar(ROWS(R_M1), pscr[0:1, :T],
                                                 1.0 / D, None, ALU.mult))
        p.wait("tensor", tk_m1r)
        tk_m1b = rank1(bcA[:, :T], R_M1)
        p.wait("vector", tk_m1b)
        tk_xc, tk_s2 = {}, {}
        for c in range(NCH):
            sq_c = sqs[:, 512 * (c % 2):512 * (c % 2) + T]
            if c >= 2:
                p.wait("vector", tk_s2[c - 2])
            tk_xc[c] = p.vop(lambda v, c=c: v.tensor_sub(XCC(c), XCC(c), bcA[:, :T]))
            tk_sq = p.vop(lambda v, c=c, sq_c=sq_c: v.tensor_mul(sq_c, XCC(c), XCC(c)))
            p.wait("tensor", tk_sq)
            tk_s2[c] = p.top(lambda t, c=c, sq_c=sq_c: t.matmul(
                pscr[0:1, :T], onrsb[:], sq_c, start=(c == 0), stop=(c == NCH - 1)))
        for c in range(NCH):
            p.wait("tensor", tk_xc[c])
            tk_sxc = p.top(lambda t, c=c: t.matmul(
                bcB[0:1, :T], onrsb[:], XCC(c), start=(c == 0), stop=(c == NCH - 1)))
        p.wait("scalar", tk_s2[NCH - 1])
        p.sop(lambda s: s.activation(ROWS(R_T1), pscr[0:1, :T], AF.Ln,
                                     bias=epsc[RBASE(R_T1):RBASE(R_T1) + 1],
                                     scale=1.0 / D))
        tk_r1 = p.sop(lambda s: s.activation(ROWS(R_R1), ROWS(R_T1), AF.Exp,
                                             scale=-0.5))
        p.wait("vector", tk_r1)
        p.wait("vector", tk_sxc)
        p.vop(lambda v: v.tensor_mul(ROWS(R_T2), ROWS(R_R1), bcB[0:1, :T]))
        p.vop(lambda v: v.tensor_scalar(ROWS(R_T2), ROWS(R_T2), 1.0 / D, None,
                                        ALU.mult))
        p.vop(lambda v: v.tensor_mul(ROWS(R_T3), ROWS(R_R1), ROWS(R_R1)))
        p.vop(lambda v: v.tensor_mul(ROWS(R_T3), ROWS(R_T3), pscr[0:1, :T]))
        p.vop(lambda v: v.tensor_scalar(ROWS(R_T3), ROWS(R_T3), 1.0 / D, None,
                                        ALU.mult))
        p.vop(lambda v: v.tensor_mul(ROWS(R_T1), ROWS(R_T2), ROWS(R_T2)))
        tk = p.vop(lambda v: v.tensor_sub(ROWS(R_T3), ROWS(R_T3), ROWS(R_T1)))
        p.wait("scalar", tk)
        p.sop(lambda s: s.activation(ROWS(R_T1), ROWS(R_T3), AF.Ln,
                                     bias=epsc[RBASE(R_T1):RBASE(R_T1) + 1],
                                     scale=1.0))
        tk_r2 = p.sop(lambda s: s.activation(ROWS(R_T3), ROWS(R_T1), AF.Exp,
                                             scale=-0.5))
        p.wait("vector", tk_r2)
        p.vop(lambda v: v.tensor_mul(ROWS(R_RR), ROWS(R_R1), ROWS(R_T3)))
        tk_rows = p.vop(lambda v: v.tensor_mul(ROWS(R_MM), ROWS(R_T2), ROWS(R_T3)))
        p.wait("tensor", tk_rows)
        tk_rrb = rank1(bcA[:, :T], R_RR)
        tk_mmb = rank1(bcB[:, :T], R_MM)
        p.wait("vector", tk_rrb)
        p.wait("vector", tk_mmb)
        tk_z = {}
        for c in range(NCH):
            ts_c = (tsc if c % 2 == 0 else tscB)[:, :T]
            if c >= 2:
                p.wait("vector", tk_z[c - 2])
            p.vop(lambda v, c=c, ts_c=ts_c: v.tensor_mul(ts_c, XCC(c), bcA[:, :T]))
            tk_sb = p.vop(lambda v, ts_c=ts_c: v.tensor_sub(
                ts_c, ts_c, bcB[:, :T]))
            p.wait("scalar", tk_sb)
            tk_z[c] = p.sop(lambda s, c=c, ts_c=ts_c: s.activation(
                ZCH(c), ts_c, AF.Identity, bias=betsb[:, c:c + 1],
                scale=gamsb[:, c:c + 1]))
        tk_ms2 = None
        tk_z_all = tk_z[NCH - 1]
        snap(3)

        def proj(wsrc, evac, pre_T, pre_sync, predma=None, wf32=None):
            p.waitl("tensor", pre_T + [tk_z_all])
            tk_w = {}
            tk_ch = {}

            def wdma(h):
                fC, pair = h // 2, h % 2
                if it == 0 and wf32 is not None:
                    tk_w[h] = qkvo_fetch(wsrc, wf32, h, [tk_ch.get(h - 2)])
                    return
                src = wsrc[fC].rearrange("c p f -> p c f")[
                    :, :, 256 * pair:256 * pair + 256]
                dst = whalf[h % 2][:].rearrange("p (c f) -> p c f", f=256)
                tk_w[h] = p.dma("sync", dst, src,
                                tick="DW" if h % 2 == 0 else "DG")

            if predma is not None:
                tk_w[0], tk_w[1] = predma
            else:
                p.waitl("sync", pre_sync + [p.now("T")])
                wdma(0)
                wdma(1)
            evt = {}
            for h in range(8):
                fC, pair = h // 2, h % 2
                p.wait("tensor", tk_w[h])
                for fs_l in range(2):
                    fc = 4 * fC + 2 * pair + fs_l
                    pa = accA if fc % 2 == 0 else accB
                    if fc - 2 in evt:
                        p.wait("tensor", evt[fc - 2])
                    for a in range(16):
                        lhs = whalf[h % 2][:, 256 * a + 128 * fs_l:
                                           256 * a + 128 * fs_l + 128]
                        tk_mm = p.top(lambda t, lhs=lhs, a=a, pa=pa: t.matmul(
                            pa[:, :T], lhs, ZCH(a), start=(a == 0), stop=(a == 15)))
                    tk_ch[h] = tk_mm
                    evt[fc] = evac(fc, pa, tk_mm)
                if h + 2 < 8:
                    p.wait("sync", tk_ch[h])
                    wdma(h + 2)
            p.wait("tensor", evt[15])
            return evt[15]

        ev_toks = {}

        def mk_evac(buf, eng="vector"):
            def evac(fc, pa, tk_mm):
                dst = buf[:, 512 * fc:512 * fc + T]
                p.wait(eng, tk_mm)
                if eng == "vector":
                    ev_toks[fc] = p.vop(lambda v, dst=dst, pa=pa:
                                        v.tensor_copy(dst, pa[:, :T]))
                else:
                    ev_toks[fc] = p.sop(lambda sx, dst=dst, pa=pa:
                                        sx.copy(dst, pa[:, :T]))
                return ev_toks[fc]
            return evac

        vperm = v_alt.rearrange("p (u a g j) -> p u g a j", a=4, g=4, j=32)

        def evac_v(fc, pa, tk_mm):
            d_up = vperm[:, 0:T // 16, :, :, fc]
            d_lo = vperm[:, 0:T // 16, :, :, fc + 16]
            src = pa[0:64, :T].rearrange("p (u g a) -> p u g a", g=4, a=4)
            src2 = pa[64:128, :T].rearrange("p (u g a) -> p u g a", g=4, a=4)
            p.wait("scalar", tk_mm)
            p.sop(lambda sx, d_up=d_up, src=src: sx.copy(d_up, src))
            ev_toks[fc] = p.sop(lambda sx, d_lo=d_lo, src2=src2:
                                sx.copy(d_lo, src2))
            return ev_toks[fc]

        def rope(buf, ev):
            p.waitl("tensor", [ev[NCH - 1], tk_sin, tk_cos, p.now("V"), p.now("S")])
            tk6 = None
            tkce = tkse = None
            for c in range(NCH):
                ch = buf[:, 512 * c:512 * c + T]
                lhsE = ecssb[:, 128 * c:128 * c + 128]
                if tk6 is not None:
                    p.waitl("tensor", [tk6, tkce, tkse])
                tkc = p.top(lambda t, lhsE=lhsE: t.matmul(
                    bcA[:, :T], lhsE, cossb[:], start=True, stop=True))
                tks = p.top(lambda t, lhsE=lhsE: t.matmul(
                    bcB[:, :T], lhsE, sinsb[:], start=True, stop=True))
                tkp = p.top(lambda t, ch=ch: t.matmul(
                    pscr[:, :T], pmsb[:], ch, start=True, stop=True))
                p.wait("scalar", tkc)
                tkce = p.sop(lambda s: s.copy(cexp[:, :T], bcA[:, :T]))
                p.wait("scalar", tks)
                tkse = p.sop(lambda s: s.copy(sexp[:, :T], bcB[:, :T]))
                p.wait("vector", tkp)
                p.vop(lambda v: v.tensor_copy(pq_sc[:, :T], pscr[:, :T]))
                p.wait("vector", tkce)
                p.vop(lambda v, ch=ch: v.tensor_mul(u2ps[:, :T], ch, cexp[:, :T]))
                p.wait("vector", tkse)
                p.vop(lambda v, ch=ch: v.tensor_mul(ch, pq_sc[:, :T], sexp[:, :T]))
                tk6 = p.vop(lambda v, ch=ch: v.tensor_add(ch, ch, u2ps[:, :T]))
            return tk6

        arE_prev = p.now("T")
        proj(wqb, mk_evac(q_buf), [], [], predma=q_pre, wf32=wq)
        q_ev = dict(ev_toks); ev_toks.clear()
        tk_qrope = rope(q_buf, q_ev)
        proj(wkb, mk_evac(k_buf, "scalar"), [], [], wf32=wk)
        k_ev = dict(ev_toks); ev_toks.clear()
        tk_krope = rope(k_buf, k_ev)
        proj(wvb, evac_v, [], [], wf32=wv)
        v_ev = dict(ev_toks); ev_toks.clear()
        snap(4)
        snap(5)

        fm_toks = []
        prev_scores_reader = None
        prev_sc = {}
        prev_PT_reader = None
        for half in range(NHALF):
            sb_done = []
            for Gl in range(GPH):
                G = half * GPH + Gl
                SG, MG = S_grp, M_grp
                G2 = 0
                tbase = 64 * G
                pack_toks = []
                ppr = prev_sc.get(0)
                p.waitl("vector", [tk_qrope, tk_krope, ppr, tk_ms2])
                p.waitl("gpsimd", [tk_qrope, tk_krope, ppr, tk_ms2])
                for slot in range(4):
                    scol = [0, 32, 64, 96][slot]
                    mcol = [0, 0, 32, 32][slot]
                    for ih in range(2):
                        srcq = q_buf[64 * ih:64 * ih + 64, :].rearrange(
                            "p (c g4 s) -> p c g4 s", g4=128, s=4)[
                            :, :, tbase // 4:tbase // 4 + 16, slot].transpose([0, 2, 1])
                        dstq = SG[64 * (slot % 2):64 * (slot % 2) + 64, :].rearrange(
                            "p (g q) -> p g q", q=128)[
                            :, :, scol + 16 * ih:scol + 16 * ih + 16]
                        pack_toks.append(p.vop(
                            lambda v, d=dstq, s=srcq: v.tensor_copy(d, s)))
                        srck = k_buf[64 * ih:64 * ih + 64, :].rearrange(
                            "p (c g4 s) -> p c g4 s", g4=128, s=4)[
                            :, :, tbase // 4:tbase // 4 + 16, slot].transpose([0, 2, 1])
                        dstk = MG[64 * (slot % 2):64 * (slot % 2) + 64, :].rearrange(
                            "p (g q) -> p g q", q=64)[
                            :, :, mcol + 16 * ih:mcol + 16 * ih + 16]
                        pack_toks.append(p.gop(
                            lambda g, d=dstk, s=srck: g.tensor_copy(d, s)))
                snap(61)
                p.waitl("tensor", pack_toks + [prev_scores_reader])
                for g in range(16):
                    p.top(lambda t, g=g: t.matmul(
                        pscr[0:64, 32 * g:32 * g + 32],
                        SG[:, 128 * g:128 * g + 64],
                        MG[:, 64 * g:64 * g + 32], start=True, stop=True))
                    tk_sc = p.top(lambda t, g=g: t.matmul(
                        pscr[64:128, 32 * g:32 * g + 32],
                        SG[:, 128 * g + 64:128 * g + 128],
                        MG[:, 64 * g + 32:64 * g + 64], start=True, stop=True))
                prev_sc[0] = tk_sc
                snap(62)
                p.wait("vector", tk_sc)
                p.waitl("vector", [prev_PT_reader])
                p.vop(lambda v: v.reduce_max(
                    smax[:], pscr[:].rearrange("p (g j) -> p g j", j=32), axis=AX.X))
                tk_sub = p.vop(lambda v: v.tensor_sub(
                    sc_sc[:].rearrange("p (g j) -> p g j", j=32),
                    pscr[:].rearrange("p (g j) -> p g j", j=32),
                    smax[:, :, None].broadcast_to([128, 16, 32])))
                prev_scores_reader = tk_sub
                p.wait("scalar", tk_sub)
                tk_exp = p.sop(lambda s: s.activation(sc_sc[:], sc_sc[:], AF.Exp,
                                                      scale=0.125))
                p.wait("vector", tk_exp)
                p.vop(lambda v: v.reduce_sum(
                    ssum[:], sc_sc[:].rearrange("p (g j) -> p g j", j=32), axis=AX.X))
                p.vop(lambda v: v.reciprocal(srec[:], ssum[:]))
                tk_pr = p.vop(lambda v: v.tensor_mul(
                    P_grp[:].rearrange("p (g j) -> p g j", j=32),
                    sc_sc[:].rearrange("p (g j) -> p g j", j=32),
                    srec[:, :, None].broadcast_to([128, 16, 32])))
                if it == 0:
                    pump(1)
                snap(63)
                if DBG:
                    p.wait("scalar", tk_pr)
                    tk_dp = p.dma("scalar", dbg_p[it, G][:], P_grp[:], tick="DXP")
                    p.waitl("vector", [tk_dp])
                tk_pve = None
                for u in range(4):
                    ug = 4 * Gl + u
                    p.waitl("tensor", [tk_pr])
                    tk_t1 = p.top(lambda t, u=u: t.transpose(
                        bfA[:, 128 * u:128 * u + 128],
                        P_grp[:, 128 * u:128 * u + 128], idbsb[:]))
                    prev_PT_reader = tk_t1
                    p.wait("vector", tk_t1)
                    for g_ in range(4):
                        src = bfA[32 * g_:32 * g_ + 32,
                                  128 * u:128 * u + 128].rearrange(
                            "p (a i) -> p a i", i=32)
                        dst = P4sb[32 * g_:32 * g_ + 32, :].rearrange(
                            "p (a w) -> p a w", w=128)[:, :, 32 * g_:32 * g_ + 32]
                        tk_pt = p.vop(lambda v, src=src, dst=dst:
                                      v.tensor_copy(dst, src))
                    snap(631)
                    if tk_pve is not None:
                        p.wait("tensor", tk_pve)
                    p.waitl("tensor", [v_ev[NCH - 1]])
                    for m_ in range(8):
                        pos0 = 16 * u + 2 * m_
                        src = v_alt[:, 32 * (64 * G + pos0):
                                    32 * (64 * G + pos0) + 64]
                        tk_vt = p.top(lambda t, m_=m_, src=src: t.transpose(
                            bfB[0:64, 64 * m_:64 * m_ + 64],
                            src, idbsb[0:64, 0:64]))
                    p.wait("vector", tk_vt)
                    for par in range(2):
                        vsrc = bfB[0:64, 0:512].rearrange(
                            "p (m c) -> p m c", c=64)[:, par::2, :]
                        vdst = V4sb[64 * par:64 * par + 64, :].rearrange(
                            "p (a c) -> p a c", c=64)
                        tk_vp = p.vop(lambda v, vsrc=vsrc, vdst=vdst:
                                      v.tensor_copy(vdst, vsrc))
                    snap(632)
                    p.waitl("tensor", [tk_pt, tk_vp])
                    for a_ in range(4):
                        tk_pv = p.top(lambda t, a_=a_: t.matmul(
                            accB[:, 64 * a_:64 * a_ + 64],
                            P4sb[:, 128 * a_:128 * a_ + 128],
                            V4sb[:, 64 * a_:64 * a_ + 64],
                            start=True, stop=True))
                    p.wait("vector", tk_pv)
                    tk_pve = p.vop(lambda v: v.tensor_copy(pv_sb[:], accB[:, 0:256]))
                    snap(633)
                    p.wait("tensor", tk_pve)
                    if it == 0:
                        pump(2)
                    for hh in range(2):
                        tk_dt = p.top(lambda t, hh=hh: t.transpose(
                            bfA[:, 512 + 128 * hh:512 + 128 * hh + 128],
                            pv_sb[:, 128 * hh:128 * hh + 128], idbsb[:]))
                    p.wait("vector", tk_dt)
                    sb_done.append(p.vop(lambda v, ug=ug: v.tensor_copy(
                        sbufT[:, 256 * ug:256 * ug + 256], bfA[:, 512:768])))
                    p.wait("tensor", sb_done[-1])
                    snap(64)
            lastT = p.now("T")
            p.waitl("vector", sb_done + [lastT])
            p.waitl("gpsimd", sb_done + [lastT])
            p.waitl("scalar", sb_done + [lastT])
            for ac in range(16):
                for hh2 in range(2):
                    i_ = ac + 16 * hh2
                    for qp in range(2):
                        for hh in range(2):
                            src = sbufT[64 * qp:64 * qp + 64, :].rearrange(
                                "p (ug hb s i) -> p ug hb s i", hb=2, s=4, i=32)[
                                :, 0:UGH, hh, :, i_]
                            dst = attnfm[64 * hh2:64 * hh2 + 64, :].rearrange(
                                "p (c t) -> p c t", t=512)[:, :, 0:T].rearrange(
                                "p c (hb ug q s) -> p c hb ug q s",
                                hb=NHALF, ug=UGH, q=4, s=4)[
                                :, ac, half, :, :, 2 * hh + qp]
                            k3 = (2 * ac + hh2) % 3
                            if k3 == 0:
                                fm_toks.append(p.vop(
                                    lambda e, s=src, d=dst: e.tensor_copy(d, s)))
                            elif k3 == 1:
                                fm_toks.append(p.sop(
                                    lambda e, s=src, d=dst: e.copy(d, s)))
                            else:
                                fm_toks.append(p.gop(
                                    lambda e, s=src, d=dst: e.tensor_copy(d, s)))
            snap(65)

        snap(6)
        if DBG:
            p.waitl("scalar", [tk_qrope, tk_krope, v_ev[NCH - 1]] + fm_toks)
            p.dma("scalar", dbg_q[it][:],
                  q_buf.rearrange("p (c t) -> p c t", t=512)[:, :, 0:T], tick="DXP")
            p.dma("scalar", dbg_k[it][:],
                  k_buf.rearrange("p (c t) -> p c t", t=512)[:, :, 0:T], tick="DXP")
            p.dma("scalar", dbg_v[it][:],
                  v_alt.rearrange("p (j t) -> p j t", t=512)[:, :, 0:T], tick="DXP")
            tk_dbg = p.dma("scalar", dbg_fm[it][:],
                           attnfm.rearrange("p (c t) -> p c t", t=512)[:, :, 0:T],
                           tick="DXP")
            p.waitl("vector", [tk_dbg])
            p.waitl("gpsimd", [tk_dbg])
            p.waitl("tensor", [tk_dbg])

        p.waitl("tensor", fm_toks)
        p.waitl("sync", [p.now("T")])
        tk_xpw = {}
        tk_wo = {}
        tk_evo = {}
        tk_cho = {}

        def wodma(h):
            if it == 0:
                tk_wo[h] = qkvo_fetch(wob, wo, h, [tk_cho.get(h - 2)])
                pump(2)
                return
            fC, pair = h // 2, h % 2
            src = wob[fC].rearrange("c p f -> p c f")[
                :, :, 256 * pair:256 * pair + 256]
            dst = whalf[h % 2][:].rearrange("p (c f) -> p c f", f=256)
            tk_wo[h] = p.dma("sync", dst, src,
                             tick="DW" if h % 2 == 0 else "DG")

        wodma(0)
        wodma(1)
        for h in range(8):
            fC, pair = h // 2, h % 2
            p.wait("tensor", tk_wo[h])
            for fs_l in range(2):
                dc = 4 * fC + 2 * pair + fs_l
                pa = (accA, accB, bcA, bcB)[dc % 4]
                if dc - 2 in tk_evo:
                    p.wait("scalar", tk_evo[dc - 2])
                tk_xd = p.dma("scalar", xbuf(dc), xsl[:, dc, :],
                              tick="DX0" if dc % 2 == 0 else "DX1")
                if dc - 4 in tk_evo:
                    p.wait("tensor", tk_evo[dc - 4])
                for a in range(16):
                    lhs = whalf[h % 2][:, 256 * a + 128 * fs_l:
                                       256 * a + 128 * fs_l + 128]
                    rhs = attnfm[:, 512 * a:512 * a + T]
                    tk_mm = p.top(lambda t, lhs=lhs, rhs=rhs, a=a, pa=pa: t.matmul(
                        pa[:, :T], lhs, rhs, start=(a == 0), stop=(a == 15)))
                tk_cho[h] = tk_mm
                p.wait("vector", tk_mm)
                p.wait("vector", tk_xd)
                if dc >= 2:
                    p.wait("vector", tk_xpw[dc - 2])
                tk_evo[dc] = p.vop(lambda v, dc=dc, pa=pa: v.tensor_add(
                    STG(dc), pa[:, :T], xbuf(dc)))
                p.wait("scalar", tk_evo[dc])
                tk_xpw[dc] = p.dma("scalar", xpsl[:, dc, :], STG(dc),
                                   tick="DO0" if dc % 2 == 0 else "DO1")
            if h + 2 < 8:
                p.wait("sync", tk_cho[h])
                wodma(h + 2)
        tk_womm = p.now("T")
        tk_xp_all = tk_xpw[NCH - 1]
        snap(7)

        p.waitl("vector", [tk_womm])
        p.waitl("gpsimd", [tk_womm])
        p.waitl("scalar", [p.now("DO0"), p.now("DO1")])
        tk_dma3 = {}
        for c in range(min(2, NCH)):
            tk_dma3[c] = p.dma("scalar", XCC(c), xpsl[:, c, :], tick="DX0" if c % 2 == 0 else "DX1")
        for c in range(NCH):
            if it == 0:
                pump(1)
            p.wait("tensor", tk_dma3[c])
            tk_m3 = p.top(lambda t, c=c: t.matmul(
                pscr[0:1, :T], onrsb[:], XCC(c), start=(c == 0), stop=(c == NCH - 1)))
            if c + 2 < NCH:
                p.wait("scalar", tk_m3)
                tk_dma3[c + 2] = p.dma("scalar", XCC(c + 2), xpsl[:, c + 2, :],
                                       tick="DX0" if c % 2 == 0 else "DX1")
        p.wait("vector", tk_m3)
        tk_m3r = p.vop(lambda v: v.tensor_scalar(ROWS(R_M1), pscr[0:1, :T],
                                                 1.0 / D, None, ALU.mult))
        p.wait("tensor", tk_m3r)
        tk_m3b = rank1(bcA[:, :T], R_M1)
        p.wait("vector", tk_m3b)
        tk_s23 = {}
        for c in range(NCH):
            sq_c = sqs[:, 512 * (c % 2):512 * (c % 2) + T]
            if c >= 2:
                p.wait("vector", tk_s23[c - 2])
            p.vop(lambda v, c=c: v.tensor_sub(XCC(c), XCC(c), bcA[:, :T]))
            tk_sq3 = p.vop(lambda v, c=c, sq_c=sq_c: v.tensor_mul(sq_c, XCC(c), XCC(c)))
            p.wait("tensor", tk_sq3)
            tk_s23[c] = p.top(lambda t, c=c, sq_c=sq_c: t.matmul(
                pscr[0:1, :T], onrsb[:], sq_c, start=(c == 0), stop=(c == NCH - 1)))
        p.wait("scalar", tk_s23[NCH - 1])
        p.sop(lambda s: s.activation(ROWS(R_T1), pscr[0:1, :T], AF.Ln,
                                     bias=epsc[RBASE(R_T1):RBASE(R_T1) + 1],
                                     scale=1.0 / D))
        tk_r3 = p.sop(lambda s: s.activation(ROWS(R_R1), ROWS(R_T1), AF.Exp,
                                             scale=-0.5))
        p.wait("tensor", tk_r3)
        tk_r3b = rank1(bcB[:, :T], R_R1)
        prev_r3b = tk_r3b
        p.wait("vector", tk_r3b)
        tk_xn = {}
        for c in range(NCH):
            tk_xn[c] = p.vop(lambda v, c=c: v.tensor_mul(XN(c), XCC(c), bcB[:, :T]))
        tk_xn_all = tk_xn[NCH - 1]
        snap(8)

        p.waitl("tensor", [tk_xn_all])
        for a in range(16):
            tk_lg = p.top(lambda t, a=a: t.matmul(
                pscr[0:NE, :T], wrsb[:, 8 * a:8 * a + 8], XN(a),
                start=(a == 0), stop=(a == 15)))
        p.wait("scalar", tk_lg)
        tk_lgs = p.sop(lambda s: s.activation(
            lgsb[:, 0:T], pscr[0:NE, :T], AF.Identity, bias=brsb[32:32 + NE]))
        if it == 0:
            pump(8)
        p.wait("tensor", tk_lgs)
        for sub in range(NSUB):
            tk_lt = p.top(lambda t, sub=sub: t.transpose(
                pscr[0:TS_, 8 * sub:8 * sub + 8],
                lgsb[:, TS_ * sub:TS_ * sub + TS_],
                idfsb[32:32 + NE, 32:32 + NE], tile_position=(32, 0)))
        p.wait("vector", tk_lt)
        p.vop(lambda v: v.tensor_copy(lgtm[0:TS_, :], pscr[0:TS_, 0:8 * NSUB]))
        for sub in range(NSUB):
            L = lgtm[0:TS_, 8 * sub:8 * sub + 8]
            p.vop(lambda v, L=L: v.max(srt8[0:TS_, :], L))
            p.vop(lambda v, L=L: v.max_index(idx8[0:TS_, :], srt8[0:TS_, :], L))
            p.vop(lambda v: v.tensor_copy(idxf[0:TS_, :], idx8[0:TS_, :]))
            tk_ng = p.vop(lambda v: v.tensor_scalar(
                negm[0:TS_, 0:1], srt8[0:TS_, 0:1], -1.0, None, ALU.mult))
            p.wait("scalar", tk_ng)
            tk_ex = p.sop(lambda s, L=L: s.activation(esc8[0:TS_, :], L, AF.Exp,
                                                      bias=negm[0:TS_, 0:1]))
            p.wait("vector", tk_ex)
            p.vop(lambda v: v.reduce_sum(negm[0:TS_, 1:2], esc8[0:TS_, :], axis=AX.X))
            p.vop(lambda v: v.reciprocal(negm[0:TS_, 1:2], negm[0:TS_, 1:2]))
            p.vop(lambda v, sub=sub: v.tensor_copy(
                mets[0:TS_, 4 * sub:4 * sub + 1], idxf[0:TS_, 0:1]))
            p.vop(lambda v, sub=sub: v.tensor_copy(
                mets[0:TS_, 4 * sub + 1:4 * sub + 2], idxf[0:TS_, 1:2]))
            p.vop(lambda v, sub=sub: v.tensor_copy(
                mets[0:TS_, 4 * sub + 2:4 * sub + 3], negm[0:TS_, 1:2]))
            tk_d21 = p.vop(lambda v: v.tensor_sub(
                negm[0:TS_, 2:3], srt8[0:TS_, 1:2], srt8[0:TS_, 0:1]))
            p.wait("scalar", tk_d21)
            tk_e2 = p.sop(lambda s: s.activation(
                negm[0:TS_, 3:4], negm[0:TS_, 2:3], AF.Exp))
            p.wait("vector", tk_e2)
            tk_msub = p.vop(lambda v, sub=sub: v.tensor_mul(
                mets[0:TS_, 4 * sub + 3:4 * sub + 4], negm[0:TS_, 3:4],
                negm[0:TS_, 1:2]))
        p.wait("tensor", tk_msub)
        for sub in range(NSUB):
            tk_mt = {}
            for m in range(4):
                tk_mt[m] = p.top(lambda t, sub=sub, m=m: t.transpose(
                    pscr[0:1, TS_ * m:TS_ * m + TS_],
                    mets[0:TS_, 4 * sub + m:4 * sub + m + 1],
                    idfsb[0:TS_, 0:TS_]))
            p.wait("vector", tk_mt[3])
            for m, ri in ((0, R_I0), (1, R_I1), (2, R_W0), (3, R_W1)):
                tk_tr = p.vop(lambda v, sub=sub, m=m, ri=ri: v.tensor_copy(
                    ROWS(ri)[:, TS_ * sub:TS_ * sub + TS_],
                    pscr[0:1, TS_ * m:TS_ * m + TS_]))
            p.wait("tensor", tk_tr)
        tk_ws0 = p.vop(lambda v: v.tensor_add(ROWS(R_WS), ROWS(R_W0), ROWS(R_W1)))
        tk_ws = p.vop(lambda v: v.tensor_copy(wsrow64[:, 0:T], ROWS(R_WS)))
        p.waitl("tensor", [tk_ws])
        tk_i0b = p.top(lambda t: t.matmul(
            bcA[0:64, :T], onesm[RBASE(R_I0):RBASE(R_I0) + 1, 0:64], ROWS(R_I0),
            start=True, stop=True, tile_position=(RBASE(R_I0), 0)))
        tk_i1b = p.top(lambda t: t.matmul(
            bcB[0:64, :T], onesm[RBASE(R_I1):RBASE(R_I1) + 1, 0:64], ROWS(R_I1),
            start=True, stop=True, tile_position=(RBASE(R_I1), 0)))
        p.wait("vector", tk_i0b)
        tk_oh0 = p.vop(lambda v: v.tensor_scalar(ohm0[:, :T], bcA[0:64, :T], eisb[:],
                                                 None, ALU.is_equal))
        p.wait("vector", tk_i1b)
        tk_oh1 = p.vop(lambda v: v.tensor_scalar(ohm1[:, :T], bcB[0:64, :T], eisb[:],
                                                 None, ALU.is_equal))
        p.waitl("tensor", [tk_oh0, tk_oh1])
        tk_w0b = rank1(bcA[:, :T], R_W0)
        tk_w1b = rank1(bcB[:, :T], R_W1)
        p.wait("vector", tk_w0b)
        tk_wb0 = p.vop(lambda v: v.tensor_copy(wb0[:, :T], bcA[:, :T]))
        p.wait("vector", tk_w1b)
        tk_wb1 = p.vop(lambda v: v.tensor_copy(wb1[:, :T], bcB[:, :T]))

        for a in range(16):
            tk_u1 = p.top(lambda t, a=a: t.matmul(
                u2ps[0:ER, :T], a1sb[:, 64 * a:64 * a + 64], XN(a),
                start=(a == 0), stop=(a == 15)))
        p.wait("vector", tk_u1)
        tk_u1e = p.vop(lambda v: v.tensor_copy(u1sb[:, :T], u2ps[0:ER, :T]))
        tk_z10 = p.vop(lambda v: v.scalar_tensor_tensor(
            z1k0[:, :T], u1sb[:, :T], 2.0, ohm0[:, :T], ALU.mult, ALU.mult))
        tk_z11 = p.vop(lambda v: v.scalar_tensor_tensor(
            z1k1[:, :T], u1sb[:, :T], 2.0, ohm1[:, :T], ALU.mult, ALU.mult))
        snap(9)

        p.waitl("tensor", [tk_z10, tk_z11, tk_wb0, tk_wb1, tk_u1e, p.now("G")])
        p.waitl("vector", [p.now("G")])
        if it == 0:
            pump_class("w1")
            pump_class("b1")
        p.waitl("sync", [tk_z11, tk_womm, p.now("T"), p.now("V"), p.now("G")])
        if it == 0:
            p.waitl("sync", feed_done["w1"] + feed_done["b1"])
        w1bufs = [w1buf, bc16(arB[:, 0:4096])]
        tk_b1d = {}
        tk_hw = {}
        tk_w1d = {}
        tk_chain = {}
        tk_b1e = {}
        tk_h0 = {}
        tk_h1 = {}
        tk_rd1 = {}
        tk_b1h0 = {}
        tk_u2t = {}

        def w1_dma(fCn):
            tk_w1d[fCn] = p.dma(
                "sync", w1bufs[fCn % 2][:].rearrange("p (c f) -> p c f", f=512),
                w1b[fCn].rearrange("c p f -> p c f"),
                tick="DW" if fCn % 2 == 0 else "DG")

        tk_b1d[0] = p.dma("sync", b1f0[:], b1rfb[:, 0:512], tick="DB0")
        tk_b1d[1] = p.dma("sync", b1f1[:], b1rfb[:, 512:1024], tick="DB1")
        w1_dma(0)
        w1_dma(1)

        def w1_chain(fc):
            fC, fs = fc // 4, fc % 4
            if fs == 0:
                if it == 0 and fC < 8:
                    pump(4)
                p.wait("tensor", tk_w1d[fC])
            if fc - 2 in tk_b1e:
                p.wait("tensor", tk_b1e[fc - 2])
            pa = accA if fc % 2 == 0 else accB
            wb = w1bufs[fC % 2]
            for a in range(16):
                lhs = wb[:, 512 * a + 128 * fs:512 * a + 128 * fs + 128]
                tk = p.top(lambda t, lhs=lhs, a=a, pa=pa: t.matmul(
                    pa[:, :T], lhs, XN(a), start=(a == 0), stop=(a == 15)))
            tk_chain[fc] = tk
            if fs == 3 and fC + 2 < 16:
                p.wait("sync", tk_chain[fc])
                w1_dma(fC + 2)

        def emit_b1e(fc):
            p.wait("scalar", tk_chain[fc])
            if fc >= 1:
                p.wait("scalar", tk_rd1[fc - 1])
            pa = accA if fc % 2 == 0 else accB
            tk_b1e[fc] = p.sop(lambda s, fc=fc, pa=pa: s.activation(
                base1[:, :T], pa[:, :T], AF.Identity, bias=b1csb[:, fc:fc + 1]))

        def emit_gelus(fc):
            if fc >= 1:
                p.wait("scalar", tk_u2t[fc - 1])
                p.wait("scalar", tk_hw[fc - 1])
            p.wait("scalar", tk_b1h0[fc])
            tk_h0[fc] = p.sop(lambda s: s.activation(
                h0sb[:, :T], bcA[:, :T], AF.Gelu_apprx_tanh))
            p.wait("scalar", tk_rd1[fc])
            tk_h1[fc] = p.sop(lambda s: s.activation(
                h1sb[:, :T], bcB[:, :T], AF.Gelu_apprx_tanh))

        def emit_readds(fc):
            fC, fs = fc // 4, fc % 4
            b1buf = b1f0 if fC % 2 == 0 else b1f1
            p.wait("tensor", tk_b1e[fc])
            p.wait("tensor", tk_b1d[fC])
            if fc >= 1:
                p.wait("tensor", tk_h0[fc - 1])
            p.top(lambda t: t.matmul(
                bcA[:, :T], idbsb[:], base1[:, :T], start=True, stop=False))
            tk_b1h0[fc] = p.top(lambda t, fs=fs, b1buf=b1buf: t.matmul(
                bcA[:, :T], b1buf[:, 128 * fs:128 * fs + 128], z1k0[:, :T],
                start=False, stop=True))
            if fc >= 1:
                p.wait("tensor", tk_h1[fc - 1])
            p.top(lambda t: t.matmul(
                bcB[:, :T], idbsb[:], base1[:, :T], start=True, stop=False))
            tk_rd1[fc] = p.top(lambda t, fs=fs, b1buf=b1buf: t.matmul(
                bcB[:, :T], b1buf[:, 128 * fs:128 * fs + 128], z1k1[:, :T],
                start=False, stop=True))
            if fs == 3 and fC + 2 < 16:
                b1buf_n = b1f0 if (fC + 2) % 2 == 0 else b1f1
                p.wait("sync", tk_rd1[fc])
                tk_b1d[fC + 2] = p.dma(
                    "sync", b1buf_n[:],
                    b1rfb[:, 512 * (fC + 2):512 * (fC + 2) + 512],
                    tick="DB0" if (fC + 2) % 2 == 0 else "DB1")

        def emit_u2(fc):
            p.waitl("tensor", [tk_h0[fc], tk_h1[fc]])
            p.top(lambda t, fc=fc: t.matmul(
                u2ps[0:ER, :T], a2sb[:, 64 * fc:64 * fc + 64], h0sb[:, :T],
                start=(fc == 0), stop=(fc == NFC - 1)))
            tk_u2t[fc] = p.top(lambda t, fc=fc: t.matmul(
                pscr[0:ER, :T], a2sb[:, 64 * fc:64 * fc + 64], h1sb[:, :T],
                start=(fc == 0), stop=(fc == NFC - 1)))

        def emit_hw(fc):
            hw_fc = hw[:, T * fc:T * fc + T]
            if it == 0 and fc == 32:
                p.waitl("vector", [feed["ce"][0], feed["ce"][1]])
                p.waitl("vector", [feed["pb"][0], feed["pb"][1]])
            p.waitl("vector", [tk_h0[fc], tk_h1[fc]])
            p.vop(lambda v: v.tensor_mul(tsc[:, :T], h0sb[:, :T], wb0[:, :T]))
            p.vop(lambda v, hw_fc=hw_fc: v.tensor_mul(
                hw_fc, h1sb[:, :T], wb1[:, :T]))
            tk_hw[fc] = p.vop(lambda v, hw_fc=hw_fc: v.tensor_add(
                hw_fc, hw_fc, tsc[:, :T]))

        w1_chain(0)
        for fc in range(NFC):
            if fc + 1 < NFC:
                w1_chain(fc + 1)
            emit_b1e(fc)
            if fc >= 1:
                emit_u2(fc - 1)
                emit_hw(fc - 1)
            emit_readds(fc)
            emit_gelus(fc)
        emit_u2(NFC - 1)
        emit_hw(NFC - 1)
        tk_u2 = tk_u2t[NFC - 1]
        tk_hw_all = tk_hw[NFC - 1]
        snap(10)

        p.wait("vector", tk_u2)
        zs0 = zsc0[:, 0:T]
        tk_wk0 = p.top(lambda t: t.matmul(
            bcA[0:64, :T], onesm[RBASE(R_W0):RBASE(R_W0) + 1, 0:64], ROWS(R_W0),
            start=True, stop=True, tile_position=(RBASE(R_W0), 0)))
        tk_wk1 = p.top(lambda t: t.matmul(
            bcB[0:64, :T], onesm[RBASE(R_W1):RBASE(R_W1) + 1, 0:64], ROWS(R_W1),
            start=True, stop=True, tile_position=(RBASE(R_W1), 0)))
        p.vop(lambda v: v.scalar_tensor_tensor(
            zs0, u2ps[0:ER, :T], 2.0, ohm0[:, :T], ALU.mult, ALU.mult))
        p.wait("vector", tk_wk0)
        p.vop(lambda v: v.tensor_mul(zs0, zs0, bcA[0:64, :T]))
        p.vop(lambda v: v.scalar_tensor_tensor(
            z2wsb[:, :T], pscr[0:ER, :T], 2.0, ohm1[:, :T], ALU.mult, ALU.mult))
        p.wait("vector", tk_wk1)
        p.vop(lambda v: v.tensor_mul(z2wsb[:, :T], z2wsb[:, :T], bcB[0:64, :T]))
        tk_z2w = p.vop(lambda v: v.tensor_add(z2wsb[:, :T], z2wsb[:, :T], zs0))
        snap(111)

        if it == 0:
            if T == 512:
                assert not feed_pend, f"feeder leftover {len(feed_pend)}"
            pump_class("w2")
        p.waitl("tensor", [tk_z2w, tk_hw_all, tk_ws])
        p.waitl("sync", [tk_xn_all, p.now("T"), p.now("V")])
        if it == 0:
            p.waitl("sync", feed_done["w2"])
        out_toks = {}
        evd = {}
        last_w2_reader = None
        for dC in range(8):
            if last_w2_reader is not None:
                p.wait("sync", last_w2_reader)
            tk_w = p.dma("sync", w2buf[:].rearrange("p (c f) -> p c f", f=256), w2b[dC].rearrange("c p f -> p c f"),
                         tick="DW")
            p.wait("tensor", tk_w)
            snap(112)
            for dh in range(2):
                dc = 2 * dC + dh
                pa = (accA, accB, bcA, bcB)[dc % 4]
                if dc - 4 in evd:
                    p.wait("tensor", evd[dc - 4])
                snap(113)
                p.top(lambda t, dc=dc, pa=pa: t.matmul(
                    pa[:, :T], b2sb[0:65, 128 * dc:128 * dc + 128],
                    bc16(arD[0:65, 2304:2560])[:, :T],
                    start=True, stop=False))
                snap(114)
                for fc in range(NFC):
                    lhs = w2buf[:, 256 * fc + 128 * dh:256 * fc + 128 * dh + 128]
                    tk_mm = p.top(lambda t, lhs=lhs, fc=fc, pa=pa: t.matmul(
                        pa[:, :T], lhs, hw[:, T * fc:T * fc + T],
                        start=False, stop=(fc == NFC - 1)))
                last_w2_reader = tk_mm
                snap(115)
                if dc - 2 in evd:
                    p.wait("scalar", evd[dc - 2])
                tk_xd = p.dma("scalar", xbuf(dc), xpsl[:, dc, :], tick="DX0" if dc % 2 == 0 else "DX1")
                p.wait("vector", tk_mm)
                p.wait("vector", tk_xd)
                if dc >= 2:
                    p.wait("vector", out_toks[dc - 2])
                evd[dc] = p.vop(lambda v, dc=dc, pa=pa: v.tensor_add(
                    STG(dc), pa[:, :T], xbuf(dc)))
                p.wait("scalar", evd[dc])
                out_toks[dc] = p.dma("scalar", outsl[:, dc, :], STG(dc), tick="DO0" if dc % 2 == 0 else "DO1")
                snap(116)

    if snaps:
        qlen, cnts = snaps[0]
        for e in list(p.q):
            p.q[e] = p.q[e][:qlen[e]]
        p.cnt = cnts
        p.join_all()
        tk_ms = p.vop(lambda v: v.memset(arX[:, 0:512], 0.0))
        p.wait("scalar", tk_ms)
        for it in range(NT):
            outsl_ = out.rearrange("(c p) t -> p c t", p=128)[:, :, it * T:it * T + T]
            for c in range(NCH):
                p.dma("scalar", outsl_[:, c, :], arX[:, 0:T],
                      tick="DO0" if c % 2 == 0 else "DO1")
    for sname in ("DO0", "DO1"):
        if p.cnt[sname]:
            p.wait("sync", (sname, p.cnt[sname]))
    p.play()
    nc._keepalive = stx
    return nc


def prep_core_inputs(inputs, N_core, core_id, T):
    f32 = np.float32
    x = np.asarray(inputs["L0_attn_hidden_states"])
    x = x.reshape(-1, x.shape[-1])
    rows = slice(core_id * N_core, (core_id + 1) * N_core)
    d = {}
    d["xT"] = np.ascontiguousarray(np.asarray(x[rows]).T.astype(f32))

    def perm_cols(w):
        return np.ascontiguousarray(
            np.asarray(w).reshape(D, 32, 64)[:, HEAD_ORDER, :].reshape(D, D))

    def tile4(w):
        return np.ascontiguousarray(w.reshape(16, 128, 4, 512).transpose(2, 0, 1, 3))

    d["wq"] = tile4(perm_cols(inputs["L0_attn_w_q"]))
    d["wk"] = tile4(perm_cols(inputs["L0_attn_w_k"]))
    d["wv"] = tile4(perm_cols(inputs["L0_attn_w_v"]))
    wo_p = np.ascontiguousarray(
        np.asarray(inputs["L0_attn_w_o"]).reshape(32, 64, D)[HEAD_ORDER].reshape(D, D))
    d["wo"] = tile4(wo_p)
    w1m = np.asarray(inputs["L0_ffn_W1"])
    d["w1"] = np.ascontiguousarray(
        np.asarray(w1m).T.reshape(16, 128, 16, 512).transpose(2, 0, 1, 3))
    w2m = np.asarray(inputs["L0_ffn_W2"])
    d["w2"] = np.ascontiguousarray(
        np.asarray(w2m).T.reshape(64, 128, 8, 256).transpose(2, 0, 1, 3))
    a1 = np.asarray(inputs["L0_ffn_A1"]).reshape(ER, D)
    d["a1t"] = np.ascontiguousarray(np.asarray(a1).T.reshape(16, 128, ER))
    d["b1rf"] = np.ascontiguousarray(
        np.asarray(inputs["L0_ffn_B1"]).transpose(0, 2, 1).reshape(ER, FF))
    a2 = np.asarray(inputs["L0_ffn_A2"]).reshape(ER, FF)
    d["a2t"] = np.ascontiguousarray(np.asarray(a2).T.reshape(64, 128, ER))
    d["b2rf"] = np.ascontiguousarray(
        np.asarray(inputs["L0_ffn_B2"]).transpose(0, 2, 1).reshape(ER, D))
    wr = np.asarray(inputs["L0_ffn_W_r"])
    d["wrt"] = np.ascontiguousarray(np.asarray(wr).T.reshape(16, 128, NE))
    d["b1c"] = np.ascontiguousarray(
        np.asarray(inputs["L0_ffn_b1"]).reshape(NFC, 128).T.astype(f32))
    b2 = np.asarray(inputs["L0_ffn_b2"]).astype(f32)
    d["b2c4d"] = np.ascontiguousarray(b2.reshape(1, 2048))
    d["brrow"] = np.ascontiguousarray(
        np.asarray(inputs["L0_ffn_b_r"]).reshape(NE, 1).astype(f32))
    d["gam"] = np.ascontiguousarray(
        np.asarray(inputs["L0_attn_ln_gamma"]).reshape(NCH, 128).T.astype(f32))
    d["bet"] = np.ascontiguousarray(
        np.asarray(inputs["L0_attn_ln_beta"]).reshape(NCH, 128).T.astype(f32))
    d["identb"] = np.eye(128, dtype=ml_dtypes.bfloat16)
    d["identf"] = np.eye(128, dtype=f32)
    pm = np.zeros((64, 64), dtype=f32)
    for dd in range(32):
        pm[2 * dd + 1, dd] = -1.0
    for dd in range(32, 64):
        pm[2 * (dd - 32), dd] = 1.0
    pmat = np.zeros((128, 128), dtype=f32)
    pmat[0:64, 0:64] = pm
    pmat[64:128, 64:128] = pm
    d["pmat"] = pmat.astype(ml_dtypes.bfloat16)
    ecs = np.zeros((32, 2048), dtype=f32)
    for ac in range(16):
        ecs[ac, 128 * ac:128 * ac + 64] = 1.0
        ecs[ac + 16, 128 * ac + 64:128 * ac + 128] = 1.0
    d["ecs"] = ecs.astype(ml_dtypes.bfloat16)
    d["onesmd"] = np.ones((128, 128), dtype=f32)
    d["onesred"] = np.ones((128, 1), dtype=f32)
    d["eiota"] = (np.arange(64) // 8).astype(f32).reshape(64, 1)
    d["invf32"] = (1.0 / 10000.0 ** (np.arange(0, 64, 2).astype(f32) / f32(64.0))
                   ).astype(f32).reshape(32, 1)
    d["toff"] = np.full((32, 1), core_id * N_core, dtype=f32)
    d["epscol"] = np.full((128, 1), 1e-5, dtype=f32)
    d["iota32d"] = np.tile(np.arange(512, dtype=f32), (32, 1))
    return d


N_CORES = 8
N_TOK_TOTAL = 16384
N_CORE = N_TOK_TOTAL // N_CORES
TILE = 512

_NC_CACHE = {}


def _get_nc():
    if "nc" not in _NC_CACHE:
        nc = build(N_CORE, TILE)
        nc.finalize()
        _NC_CACHE["nc"] = nc
    return _NC_CACHE["nc"]


def kernel(**inputs):
    from concourse.bass_utils import run_bass_kernel_spmd
    nc = _get_nc()
    shared = prep_core_inputs(inputs, N_CORE, 0, TILE)
    xfull = np.asarray(inputs["L0_attn_hidden_states"])
    xfull = np.asarray(xfull).reshape(-1, D)
    in_maps = []
    for c in range(N_CORES):
        d = dict(shared)
        rows = slice(c * N_CORE, (c + 1) * N_CORE)
        d["xT"] = np.ascontiguousarray(np.asarray(xfull[rows]).T.astype(np.float32))
        d["toff"] = np.full((32, 1), c * N_CORE, dtype=np.float32)
        in_maps.append(d)
    res = run_bass_kernel_spmd(nc, in_maps, core_ids=list(range(N_CORES)))
    out = np.empty((N_TOK_TOTAL, D), dtype=np.float32)
    for c in range(N_CORES):
        out[c * N_CORE:(c + 1) * N_CORE] = np.asarray(res.results[c]["out"]).T
    return out.reshape(1, N_TOK_TOTAL, D)


# version 24 (final)
# speedup vs baseline: 1.0462x; 1.0462x over previous
import math
import contextlib
import numpy as np
import ml_dtypes
import concourse.bass as bass
import concourse.mybir as mybir

F32 = mybir.dt.float32
BF16 = mybir.dt.bfloat16
U32 = mybir.dt.uint32
AF = mybir.ActivationFunctionType
ALU = mybir.AluOpType
AX = mybir.AxisListType

D = 2048
FF = 8192
NE, NR = 8, 8
ER = NE * NR
EPS = 1e-5
NCH = 16
NFC = 64

TWO_PI = 2.0 * math.pi
CW1 = 6.28125
CW2 = float(np.float32(TWO_PI - CW1))
CW3 = float(np.float64(TWO_PI) - CW1 - np.float64(np.float32(TWO_PI - CW1)))

HEAD_ORDER = [(a + 16 * h) for a in range(16) for h in range(2)]


class Prog:
    def __init__(self, nc, sem_names):
        self.nc = nc
        self.q = {e: [] for e in ("sync", "tensor", "vector", "scalar", "gpsimd")}
        self.kinds = {e: [] for e in self.q}
        self.cnt = {s: 0 for s in sem_names}
        self.sem_names = list(sem_names)

    def _emit_wait(self, eng, s, v):
        self.kinds[eng].append("w")
        self.q[eng].append(lambda e, sems, s=s, v=v: e.wait_ge(sems[s], v))

    def wait(self, eng, tok):
        if tok is not None:
            self._emit_wait(eng, tok[0], tok[1])

    def waitl(self, eng, toks):
        best = {}
        for t in toks:
            if t is None:
                continue
            s, v = t
            if s not in best or v > best[s]:
                best[s] = v
        for s, v in best.items():
            self._emit_wait(eng, s, v)

    def now(self, sem):
        return (sem, self.cnt[sem]) if self.cnt[sem] > 0 else None

    def _op(self, eng, fn, tick, tickn=1):
        self.kinds[eng].append("d" if tickn == 16 else "o")
        self.cnt[tick] += tickn
        tok = (tick, self.cnt[tick])

        def run(e, sems, f=fn, s=tick, n=tickn):
            f(e).then_inc(sems[s], n)
        self.q[eng].append(run)
        return tok

    def top(self, fn):
        return self._op("tensor", fn, "T")

    def vop(self, fn):
        if self.cnt["V"] > 0:
            self._emit_wait("vector", "V", self.cnt["V"])
        return self._op("vector", fn, "V")

    def sop(self, fn):
        if self.cnt["S"] > 0:
            self._emit_wait("scalar", "S", self.cnt["S"])
        return self._op("scalar", fn, "S")

    def gop(self, fn):
        if self.cnt["G"] > 0:
            self._emit_wait("gpsimd", "G", self.cnt["G"])
        return self._op("gpsimd", fn, "G")

    def dma(self, eng, dst, src, tick):
        def fn(e, d=dst, s=src):
            n0 = self.nc.n_instructions()
            ins = e.dma_start(d, s)
            n1 = self.nc.n_instructions()
            if n1 - n0 != 1:
                import traceback
                print(f"DMA SPLIT x{n1 - n0}: dst={d.tensor.name if hasattr(d, 'tensor') else d}")
                for l in traceback.format_stack()[-8:-4]:
                    print("   ", l.split(chr(10))[0])
            return ins
        return self._op(eng, fn, tick, 16)

    def join_all(self):
        toks = [self.now(s) for s in self.sem_names]
        for e in ("tensor", "vector", "scalar", "gpsimd", "sync"):
            self.waitl(e, toks)

    def play(self):
        nc = self.nc
        with contextlib.ExitStack() as stx:
            sems = {n: stx.enter_context(nc.semaphore(n)) for n in self.sem_names}
            block = stx.enter_context(nc.Block())
            for eng_name in ("sync", "tensor", "vector", "scalar", "gpsimd"):
                fns = self.q[eng_name]

                def body(e, fns=fns, sems=sems):
                    for f in fns:
                        f(e, sems)
                getattr(block, eng_name)(body)


def bc16(ap):
    return ap.bitcast(BF16)


def build(N, T):
    assert T % 64 == 0 and N % T == 0 and T <= 512
    NT = N // T
    NG = T // 64
    NHALF = 2 if NG % 2 == 0 else 1
    GPH = NG // NHALF
    UGH = 4 * GPH
    nc = bass.Bass("TRN2", target_bir_lowering=False, debug=False)

    inp = lambda name, shape, dtype=F32: nc.dram_tensor(
        name, list(shape), dtype, kind="ExternalInput").ap()
    sdram = lambda name, shape, dtype: nc.dram_tensor(
        name, list(shape), dtype, kind="Internal").ap()

    xT = inp("xT", [D, N])
    wq = inp("wq", [8, 128, 4096]); wk = inp("wk", [8, 128, 4096])
    wv = inp("wv", [8, 128, 4096]); wo = inp("wo", [8, 128, 4096])
    w1 = inp("w1", [16, 128, 8192])
    w2 = inp("w2", [8, 128, 16384])
    a1t = inp("a1t", [16, 128, 64])
    b1rf = inp("b1rf", [ER, FF])
    a2t = inp("a2t", [64, 128, 64])
    b2rf = inp("b2rf", [ER, D])
    wrt = inp("wrt", [16, 128, 8])
    b1c = inp("b1c", [128, NFC])
    b2c4_d = inp("b2c4d", [1, 2048])
    brrow_d = inp("brrow", [NE, 1])
    gam_d = inp("gam", [128, NCH]); bet_d = inp("bet", [128, NCH])
    identb_d = inp("identb", [128, 128], BF16)
    identf_d = inp("identf", [128, 128])
    pmat_d = inp("pmat", [128, 128], BF16)
    ecs_d = inp("ecs", [32, 2048], BF16)
    onesm_d = inp("onesmd", [128, 128])
    onesr_d = inp("onesred", [128, 1])
    eiota_d = inp("eiota", [64, 1])
    invf_d = inp("invf32", [32, 1])
    toff_d = inp("toff", [32, 1])
    eps_d = inp("epscol", [128, 1])
    iota_d = inp("iota32d", [32, 512])

    out = nc.dram_tensor("out", [D, N], F32, kind="ExternalOutput").ap()

    xp = sdram("xp", [D, N], F32)
    DBG = False
    PHASE = 99
    snaps = []
    if DBG:
        TT_ = T
        dbg_p = nc.dram_tensor("dbg_p", [NT, T // 64, 128, 512], BF16,
                               kind="ExternalOutput").ap()
        dbg_q = nc.dram_tensor("dbg_q", [NT, 128, 16, TT_], BF16, kind="ExternalOutput").ap()
        dbg_k = nc.dram_tensor("dbg_k", [NT, 128, 16, TT_], BF16, kind="ExternalOutput").ap()
        dbg_v = nc.dram_tensor("dbg_v", [NT, 64, 32, TT_], BF16, kind="ExternalOutput").ap()
        dbg_fm = nc.dram_tensor("dbg_fm", [NT, 128, 16, TT_], BF16, kind="ExternalOutput").ap()
    wqb = sdram("wqb", [8, 128, 4096], BF16)
    wkb = sdram("wkb", [8, 128, 4096], BF16)
    wvb = sdram("wvb", [8, 128, 4096], BF16)
    wob = sdram("wob", [8, 128, 4096], BF16)
    w1b = sdram("w1b", [16, 128, 8192], BF16)
    w2b = sdram("w2b", [8, 128, 16384], BF16)
    b1rfb = sdram("b1rfb", [ER, FF], BF16)

    p = Prog(nc, ("T", "V", "S", "G", "DW", "DX", "DG", "DW2", "DW3", "DXP",
                  "DX0", "DX1", "DO0", "DO1", "DB0", "DB1", "PA0", "PA1",
                  "PA2", "PA3", "PB0", "PB1", "PB2", "PB3"))

    stx = contextlib.ExitStack()
    sb = lambda name, shape, dtype=F32: stx.enter_context(
        nc.sbuf_tensor(name, list(shape), dtype))
    psm = lambda name, shape, dtype=F32: stx.enter_context(
        nc.psum_tensor(name, list(shape), dtype))

    arH = sb("arH", [128, 16384])
    arC = sb("arC", [128, 8192])
    arB = sb("arB", [128, 8192])
    arE = sb("arE", [128, 4096])
    arD = sb("arD", [128, 4096])
    arX = sb("arX", [128, 2048])

    a1sb = sb("a1sb", [128, 16 * 64], BF16)
    a2sb = sb("a2sb", [128, 64 * 64], BF16)
    b2sb = sb("b2sb", [ER + 1, D], BF16)
    wrsb = sb("wrsb", [128, 16 * 8], BF16)
    gamsb = sb("gamsb", [128, NCH]); betsb = sb("betsb", [128, NCH])
    b1csb = sb("b1csb", [128, NFC])
    brsb = sb("brsb", [40, 1])
    idbsb = sb("idbsb", [128, 128], BF16); idfsb = sb("idfsb", [128, 128])
    pmsb = sb("pmsb", [128, 128], BF16)
    ecssb = sb("ecssb", [32, 2048], BF16)
    onesm = sb("onesm", [128, 128])
    onrsb = sb("onrsb", [128, 1])
    eisb = sb("eisb", [64, 1]); ivsb = sb("ivsb", [32, 1]); tosb = sb("tosb", [32, 1])
    epsc = sb("epsc", [128, 1])
    cossb = sb("cossb", [32, T], BF16); sinsb = sb("sinsb", [32, T], BF16)
    rowm2 = sb("rowm2", [128, 2048])
    NSUB = max(T // 128, 1)
    TS_ = min(T, 128)
    lgtm = sb("lgtm", [128, 8 * NSUB])
    mets = sb("mets", [128, 4 * NSUB])
    srt8 = sb("srt8", [128, 8])
    idx8 = sb("idx8", [128, 8], U32)
    idxf = sb("idxf", [128, 8])
    esc8 = sb("esc8", [128, 8])
    negm = sb("negm", [128, 4])
    smax = sb("smax", [128, 16]); ssum = sb("ssum", [128, 16]); srec = sb("srec", [128, 16])

    accA = psm("accA", [128, 512])
    accB = psm("accB", [128, 512])
    bcA = psm("bcA", [128, 512])
    bcB = psm("bcB", [128, 512])
    pscr = psm("pscr", [128, 512])
    bfA = psm("bfA", [128, 1024], BF16)
    bfB = psm("bfB", [128, 1024], BF16)
    u2ps = psm("u2ps", [128, 512])

    k_buf = bc16(arH[:, 0:4096])
    sbufT = bc16(arH[:, 4096:4096 + 128 * UGH])
    hw = bc16(arH[:, 0:(NFC * T) // 2])
    z_buf = bc16(arC[:, 0:4096])
    attnfm = bc16(arC[:, 0:4096])
    q_buf = bc16(arC[:, 4096:8192])
    xn2 = bc16(arC[:, 0:4096])
    w1buf = bc16(arC[:, 4096:8192])
    xcb = arB
    v_alt = bc16(arB[0:64, :])
    w2buf = bc16(arB)
    wbuf = bc16(arE[:, 0:4096])
    S_grp = bc16(arD[:, 0:1024])
    M_grp = bc16(arD[:, 1024:1536])
    P_grp = bc16(arD[:, 1536:1792])
    P4sb = bc16(arD[:, 1792:2048])
    V4sb = bc16(arD[:, 2048:2176])
    pv_sb = bc16(arD[:, 2176:2304])
    sc_sc = arD[:, 2304:2816]
    pq_sc = bc16(arD[:, 2816:3072])
    cexp = bc16(arD[:, 3072:3328])
    sexp = bc16(arD[:, 3328:3584])
    base1 = bc16(arD[:, 0:256])
    h0sb = bc16(arD[:, 256:512])
    h1sb = bc16(arD[:, 512:768])
    wb0 = bc16(arD[:, 768:1024])
    wb1 = bc16(arD[:, 1024:1280])
    ohm0 = bc16(arD[0:64, 1280:1536])
    ohm1 = bc16(arD[0:64, 1536:1792])
    z1k0 = bc16(arD[0:64, 1792:2048])
    z1k1 = bc16(arD[0:64, 2048:2304])
    z2wsb = bc16(arD[0:64, 2304:2560])
    wsrow64 = bc16(arD[64:65, 2304:2560])
    u1sb = arD[0:64, 2560:3072]
    b1f0 = bc16(arD[0:64, 3072:3328])
    b1f1 = bc16(arD[0:64, 3328:3584])
    sqs = arH[:, 0:1024]
    tsc = arD[:, 3584:4096]
    tscB = arD[:, 3072:3584]
    whalf = [bc16(arE[:, 0:2048]), bc16(arE[:, 2048:4096])]
    zsc0 = arD[:, 3584:4096][0:64, :]
    tgr = rowm2[0:32, 0:512][:, 0:T]
    angr = rowm2[0:32, 512:1024][:, 0:T]
    an2r = tgr
    nrr = rowm2[0:32, 1024:1536][:, 0:T]
    xch0 = arX[:, 1024:1536]
    xch1 = arX[:, 1536:2048]
    lgsb = arX[32:32 + NE, 0:512]

    STG = lambda i: arX[:, 512 * (i % 2):512 * (i % 2) + T]

    R_M1, R_R1, R_RR, R_MM, R_T1, R_T2, R_T3, R_WS, R_I0, R_I1, R_W0, R_W1 = range(12)
    ROWMAP = {R_R1: (0, 0), R_T1: (0, 1), R_T2: (0, 2), R_T3: (0, 3),
              R_M1: (32, 0), R_W0: (32, 1), R_W1: (32, 2), R_WS: (32, 3),
              R_RR: (64, 0), R_MM: (64, 1), R_I0: (64, 2), R_I1: (64, 3)}

    def ROWS(i):
        b, cb = ROWMAP[i]
        return rowm2[b:b + 1, 512 * cb:512 * cb + T]

    RBASE = lambda i: ROWMAP[i][0]

    XCC = lambda c: xcb[:, 512 * c:512 * c + T]
    ZCH = lambda c: z_buf[:, 512 * c:512 * c + T]
    XN = lambda c: xn2[:, 512 * c:512 * c + T]

    def rank1(dst_ps, row_i, nrows=128):
        b = RBASE(row_i)
        return p.top(lambda t, b=b, row_i=row_i, nrows=nrows: t.matmul(
            dst_ps, onesm[b:b + 1, 0:nrows], ROWS(row_i), start=True, stop=True,
            tile_position=(b, 0)))

    SNAPN = 1
    snap_hits = {"n": 0}
    marks = []
    nc._marks = marks

    def mark(name):
        marks.append((name, {e: len(p.q[e]) for e in p.q}))

    def snap(k):
        mark(str(k))
        if PHASE == k and not snaps:
            snap_hits["n"] += 1
            if snap_hits["n"] >= SNAPN:
                snaps.append(({e: len(p.q[e]) for e in p.q}, dict(p.cnt)))

    stg_f32 = [arH[:, 4096 * i:4096 * i + 4096] for i in range(4)]
    cast_bf = [bc16(arC[:, 2048 * i:2048 * i + 2048]) for i in range(4)]
    PA = ("PA0", "PA1", "PA2", "PA3")
    PB = ("PB0", "PB1", "PB2", "PB3")

    cast_state = {"i": 0, "pb": [None] * 4, "ce": [None] * 4}

    def cast_chunk(src_f32, dst_bf16_dram, ncols=4096):
        i = cast_state["i"]; cast_state["i"] += 1
        slot = i % 4
        p.waitl("sync", [cast_state["ce"][slot]])
        tk_in = p.dma("sync", stg_f32[slot][:, 0:ncols], src_f32, tick=PA[slot])
        p.wait("vector", tk_in)
        p.waitl("vector", [cast_state["pb"][slot]])
        tk_c = p.vop(lambda v, slot=slot, ncols=ncols: v.tensor_copy(
            cast_bf[slot][:, 0:ncols], stg_f32[slot][:, 0:ncols]))
        cast_state["ce"][slot] = tk_c
        p.wait("scalar", tk_c)
        tk_out = p.dma("scalar", dst_bf16_dram, cast_bf[slot][:, 0:ncols],
                       tick=PB[slot])
        cast_state["pb"][slot] = tk_out
        return tk_out

    def cast_small(src3d, dst2d, nrows, nc_, nf_):
        i = cast_state["i"]; cast_state["i"] += 1
        slot = i % 4
        ncols = nc_ * nf_
        p.waitl("sync", [cast_state["ce"][slot]])
        stv = stg_f32[slot][0:nrows, 0:ncols].rearrange("p (c f) -> p c f", f=nf_)
        tk_in = p.dma("sync", stv, src3d, tick=PA[slot])
        p.wait("vector", tk_in)
        tk_c = p.vop(lambda v, slot=slot, nrows=nrows, ncols=ncols, dst2d=dst2d:
                     v.tensor_copy(dst2d, stg_f32[slot][0:nrows, 0:ncols]))
        cast_state["ce"][slot] = tk_c
        return tk_c

    cast_small(a1t.rearrange("c p f -> p c f"), a1sb[:], 128, 16, 64)
    cast_small(a2t.rearrange("c p f -> p c f"), a2sb[:], 128, 64, 64)
    cast_small(b2rf[:].rearrange("e (c f) -> e c f", c=1), b2sb[0:64, :], 64, 1, 2048)
    cast_small(wrt.rearrange("c p f -> p c f"), wrsb[:], 128, 16, 8)
    cast_small(b2c4_d.rearrange("e (c f) -> e c f", c=1), b2sb[64:65, :], 1, 1, 2048)
    tok_casts = ("V", p.cnt["V"])

    for dstt, srct in [(gamsb[:], gam_d[:]), (betsb[:], bet_d[:]), (b1csb[:], b1c[:]),
                       (brsb[32:32 + NE, 0:1], brrow_d[:]),
                       (idfsb[:], identf_d[:]), (onesm[:], onesm_d[:]),
                       (onrsb[:], onesr_d[:]), (eisb[:], eiota_d[:]),
                       (ivsb[:], invf_d[:]), (tosb[:], toff_d[:]),
                       (epsc[:], eps_d[:]), (idbsb[:], identb_d[:]),
                       (pmsb[:], pmat_d[:]), (ecssb[:], ecs_d[:])]:
        tok_loads = p.dma("scalar", dstt, srct, tick="DXP")
    for e in ("tensor", "vector", "scalar", "gpsimd", "sync"):
        p.wait(e, tok_loads)
        p.wait(e, tok_casts)
        for sn in PA + PB:
            p.wait(e, p.now(sn))
        p.wait(e, p.now("S"))
        p.wait(e, p.now("G"))
    snap(1)

    FSTG = [arH[:, 8192:10240], arH[:, 11264:13312]]
    FDST = [bc16(arH[:, 10240:11264]), bc16(arH[:, 13312:14336])]
    feed = {"i": 0, "ce": [None, None], "pb": [None, None]}

    def fpipe(src3, scratch3, B, F, dst_bf=None, pre_cast=()):
        slot = feed["i"] % 2; feed["i"] += 1
        p.waitl("sync", [feed["ce"][slot]])
        stv = FSTG[slot].rearrange("p (c f) -> p c f", f=F)[:, 0:B, :]
        tk_in = p.dma("sync", stv, src3, tick=PA[slot])
        eng = "vector" if (dst_bf is not None or feed["i"] % 2 == 0) else "scalar"
        p.wait(eng, tk_in)
        p.waitl(eng, [feed["pb"][slot]] + list(pre_cast))
        if dst_bf is None:
            if feed["i"] % 2 == 0:
                tk_c = p.vop(lambda v, slot=slot: v.tensor_copy(
                    FDST[slot][:], FSTG[slot][:]))
            else:
                tk_c = p.sop(lambda sx, slot=slot: sx.copy(
                    FDST[slot][:], FSTG[slot][:]))
            osrc = FDST[slot].rearrange("p (c f) -> p c f", f=F)[:, 0:B, :]
        else:
            tk_c = p.vop(lambda v, slot=slot, dst_bf=dst_bf, stv=stv:
                         v.tensor_copy(dst_bf, stv))
            osrc = dst_bf
        feed["ce"][slot] = tk_c
        p.wait("scalar", tk_c)
        tk_out = p.dma("scalar", scratch3, osrc, tick=PB[slot])
        feed["pb"][slot] = tk_out
        return tk_c, tk_out

    feed_pend = []
    feed_done = {}

    def feeder_fill():
        for fC in range(16):
            for j in range(4):
                src = w1[fC][:, 2048 * j:2048 * j + 2048].rearrange(
                    "p (c f) -> p c f", f=2048)
                scr = w1b[fC][:, 2048 * j:2048 * j + 2048].rearrange(
                    "p (c f) -> p c f", f=2048)
                feed_pend.append(("w1", src, scr, 1, 2048))
        b1s = b1rf.rearrange("e (h q) -> (e h) q", h=2)
        b1d = b1rfb.rearrange("e (h q) -> (e h) q", h=2)
        for j in range(2):
            feed_pend.append(("b1", b1s[:, 2048 * j:2048 * j + 2048].rearrange(
                "p (c f) -> p c f", f=2048),
                b1d[:, 2048 * j:2048 * j + 2048].rearrange(
                "p (c f) -> p c f", f=2048), 1, 2048))
        for dC in range(8):
            for j in range(8):
                src = w2[dC][:, 2048 * j:2048 * j + 2048].rearrange(
                    "p (c f) -> p c f", f=2048)
                scr = w2b[dC][:, 2048 * j:2048 * j + 2048].rearrange(
                    "p (c f) -> p c f", f=2048)
                feed_pend.append(("w2", src, scr, 1, 2048))

    def pump(n):
        for _ in range(n):
            if not feed_pend:
                return
            cls, src, scr, B, F = feed_pend.pop(0)
            fpipe(src, scr, B, F)
            if not feed_pend or feed_pend[0][0] != cls:
                feed_done[cls] = [feed["pb"][0], feed["pb"][1]]

    def pump_class(cls):
        while feed_pend and feed_pend[0][0] == cls:
            pump(1)

    def qkvo_fetch(wscr, wf32, h, pre):
        tc = None
        for cq in range(2):
            src = wf32[h][:, 2048 * cq:2048 * cq + 2048].rearrange(
                "p (c f) -> p c f", f=2048)
            scr = wscr[h][:, 2048 * cq:2048 * cq + 2048].rearrange(
                "p (c f) -> p c f", f=2048)
            dstv = whalf[h % 2][:, 2048 * cq:2048 * cq + 2048]
            tc, _ = fpipe(src, scr, 1, 2048, dst_bf=dstv, pre_cast=pre)
        return tc

    prev_r3b = None
    for it in range(NT):
        t0 = it * T
        xsl = xT.rearrange("(c p) t -> p c t", p=128)[:, :, t0:t0 + T]
        xpsl = xp.rearrange("(c p) t -> p c t", p=128)[:, :, t0:t0 + T]
        outsl = out.rearrange("(c p) t -> p c t", p=128)[:, :, t0:t0 + T]

        p.waitl("scalar", [prev_r3b])
        tk_io = p.dma("scalar", tgr, iota_d[:, 0:T], tick="DXP")
        p.waitl("vector", [tk_io, prev_r3b])
        p.vop(lambda v, t0=t0: v.tensor_scalar(tgr, tgr, tosb[:], float(t0),
                                               ALU.add, ALU.add))
        p.vop(lambda v: v.tensor_scalar(angr, tgr, ivsb[:], None, ALU.mult))

        nintv = u2ps.bitcast(mybir.dt.int32)[0:32, 0:T]
        dscr = pscr[0:32, 0:T]

        def range_reduce(src, dst):
            p.vop(lambda v: v.tensor_scalar(nrr, src, 1.0 / TWO_PI, None, ALU.mult))
            p.vop(lambda v: v.tensor_copy(nintv, nrr))
            p.vop(lambda v: v.tensor_copy(dst, nintv))
            p.vop(lambda v: v.tensor_sub(dscr, nrr, dst))
            p.vop(lambda v: v.tensor_scalar(nrr, dscr, 0.5, None, ALU.is_gt))
            p.vop(lambda v: v.tensor_add(dst, dst, nrr))
            p.vop(lambda v: v.tensor_scalar(nrr, dscr, -0.5, None, ALU.is_lt))
            p.vop(lambda v: v.tensor_sub(dst, dst, nrr))
            p.vop(lambda v: v.scalar_tensor_tensor(nrr, dst, -CW1, src,
                                                   ALU.mult, ALU.add))
            p.vop(lambda v: v.scalar_tensor_tensor(nrr, dst, -CW2, nrr,
                                                   ALU.mult, ALU.add))
            p.vop(lambda v: v.scalar_tensor_tensor(dst, dst, -CW3, nrr,
                                                   ALU.mult, ALU.add))
            p.vop(lambda v: v.tensor_scalar(dst, dst, math.pi, None, ALU.min))
            return p.vop(lambda v: v.tensor_scalar(dst, dst, -math.pi, None, ALU.max))

        tk = range_reduce(angr, tgr)
        p.wait("scalar", tk)
        tk_sin = p.sop(lambda s: s.activation(sinsb[:], tgr, AF.Sin))
        p.wait("vector", tk_sin)
        p.vop(lambda v: v.tensor_scalar(an2r, angr, math.pi / 2.0, None, ALU.add))
        tk = range_reduce(an2r, angr)
        p.wait("scalar", tk)
        tk_cos = p.sop(lambda s: s.activation(cossb[:], angr, AF.Sin))
        snap(2)

        p.join_all()
        p.vop(lambda v: v.memset(S_grp[:], 0.0))
        p.vop(lambda v: v.memset(P4sb[:], 0.0))
        q_pre = []
        if it == 0:
            feeder_fill()
            q_pre.append(qkvo_fetch(wqb, wq, 0, []))
            q_pre.append(qkvo_fetch(wqb, wq, 1, []))
        else:
            for h_ in range(2):
                q_pre.append(p.dma("sync", whalf[h_][:], wqb[h_][:],
                                   tick="DW" if h_ == 0 else "DG"))

        tk_dma = {}
        xbuf = lambda c: (xch0 if c % 2 == 0 else xch1)[:, 0:T]
        for c in range(min(2, NCH)):
            tk_dma[c] = p.dma("scalar", XCC(c), xsl[:, c, :], tick="DX0" if c % 2 == 0 else "DX1")
        for c in range(NCH):
            p.wait("tensor", tk_dma[c])
            tk_m1 = p.top(lambda t, c=c: t.matmul(
                pscr[0:1, :T], onrsb[:], XCC(c), start=(c == 0), stop=(c == NCH - 1)))
            if c + 2 < NCH:
                p.wait("scalar", tk_m1)
                tk_dma[c + 2] = p.dma("scalar", XCC(c + 2), xsl[:, c + 2, :],
                                      tick="DX0" if c % 2 == 0 else "DX1")
        p.wait("vector", tk_m1)
        tk_m1r = p.vop(lambda v: v.tensor_scalar(ROWS(R_M1), pscr[0:1, :T],
                                                 1.0 / D, None, ALU.mult))
        p.wait("tensor", tk_m1r)
        tk_m1b = rank1(bcA[:, :T], R_M1)
        p.wait("vector", tk_m1b)
        tk_xc, tk_s2 = {}, {}
        for c in range(NCH):
            sq_c = sqs[:, 512 * (c % 2):512 * (c % 2) + T]
            if c >= 2:
                p.wait("vector", tk_s2[c - 2])
            tk_xc[c] = p.vop(lambda v, c=c: v.tensor_sub(XCC(c), XCC(c), bcA[:, :T]))
            tk_sq = p.vop(lambda v, c=c, sq_c=sq_c: v.tensor_mul(sq_c, XCC(c), XCC(c)))
            p.wait("tensor", tk_sq)
            tk_s2[c] = p.top(lambda t, c=c, sq_c=sq_c: t.matmul(
                pscr[0:1, :T], onrsb[:], sq_c, start=(c == 0), stop=(c == NCH - 1)))
        for c in range(NCH):
            p.wait("tensor", tk_xc[c])
            tk_sxc = p.top(lambda t, c=c: t.matmul(
                bcB[0:1, :T], onrsb[:], XCC(c), start=(c == 0), stop=(c == NCH - 1)))
        p.wait("scalar", tk_s2[NCH - 1])
        p.sop(lambda s: s.activation(ROWS(R_T1), pscr[0:1, :T], AF.Ln,
                                     bias=epsc[RBASE(R_T1):RBASE(R_T1) + 1],
                                     scale=1.0 / D))
        tk_r1 = p.sop(lambda s: s.activation(ROWS(R_R1), ROWS(R_T1), AF.Exp,
                                             scale=-0.5))
        p.wait("vector", tk_r1)
        p.wait("vector", tk_sxc)
        p.vop(lambda v: v.tensor_mul(ROWS(R_T2), ROWS(R_R1), bcB[0:1, :T]))
        p.vop(lambda v: v.tensor_scalar(ROWS(R_T2), ROWS(R_T2), 1.0 / D, None,
                                        ALU.mult))
        p.vop(lambda v: v.tensor_mul(ROWS(R_T3), ROWS(R_R1), ROWS(R_R1)))
        p.vop(lambda v: v.tensor_mul(ROWS(R_T3), ROWS(R_T3), pscr[0:1, :T]))
        p.vop(lambda v: v.tensor_scalar(ROWS(R_T3), ROWS(R_T3), 1.0 / D, None,
                                        ALU.mult))
        p.vop(lambda v: v.tensor_mul(ROWS(R_T1), ROWS(R_T2), ROWS(R_T2)))
        tk = p.vop(lambda v: v.tensor_sub(ROWS(R_T3), ROWS(R_T3), ROWS(R_T1)))
        p.wait("scalar", tk)
        p.sop(lambda s: s.activation(ROWS(R_T1), ROWS(R_T3), AF.Ln,
                                     bias=epsc[RBASE(R_T1):RBASE(R_T1) + 1],
                                     scale=1.0))
        tk_r2 = p.sop(lambda s: s.activation(ROWS(R_T3), ROWS(R_T1), AF.Exp,
                                             scale=-0.5))
        p.wait("vector", tk_r2)
        p.vop(lambda v: v.tensor_mul(ROWS(R_RR), ROWS(R_R1), ROWS(R_T3)))
        tk_rows = p.vop(lambda v: v.tensor_mul(ROWS(R_MM), ROWS(R_T2), ROWS(R_T3)))
        p.wait("tensor", tk_rows)
        tk_rrb = rank1(bcA[:, :T], R_RR)
        tk_mmb = rank1(bcB[:, :T], R_MM)
        p.wait("vector", tk_rrb)
        p.wait("vector", tk_mmb)
        tk_z = {}
        for c in range(NCH):
            ts_c = (tsc if c % 2 == 0 else tscB)[:, :T]
            if c >= 2:
                p.wait("vector", tk_z[c - 2])
            p.vop(lambda v, c=c, ts_c=ts_c: v.tensor_mul(ts_c, XCC(c), bcA[:, :T]))
            tk_sb = p.vop(lambda v, ts_c=ts_c: v.tensor_sub(
                ts_c, ts_c, bcB[:, :T]))
            p.wait("scalar", tk_sb)
            tk_z[c] = p.sop(lambda s, c=c, ts_c=ts_c: s.activation(
                ZCH(c), ts_c, AF.Identity, bias=betsb[:, c:c + 1],
                scale=gamsb[:, c:c + 1]))
        tk_ms2 = None
        tk_z_all = tk_z[NCH - 1]
        snap(3)

        def proj(wsrc, evac, pre_T, pre_sync, predma=None, wf32=None):
            p.waitl("tensor", pre_T + [tk_z_all])
            tk_w = {}
            tk_ch = {}

            def wdma(h):
                if it == 0 and wf32 is not None:
                    tk_w[h] = qkvo_fetch(wsrc, wf32, h, [tk_ch.get(h - 2)])
                    return
                tk_w[h] = p.dma("sync", whalf[h % 2][:], wsrc[h][:],
                                tick="DW" if h % 2 == 0 else "DG")

            if predma is not None:
                tk_w[0], tk_w[1] = predma
            else:
                p.waitl("sync", pre_sync + [p.now("T")])
                wdma(0)
                wdma(1)
            evt = {}
            for h in range(8):
                fC, pair = h // 2, h % 2
                p.wait("tensor", tk_w[h])
                for fs_l in range(2):
                    fc = 4 * fC + 2 * pair + fs_l
                    pa = accA if fc % 2 == 0 else accB
                    if fc - 2 in evt:
                        p.wait("tensor", evt[fc - 2])
                    for a in range(16):
                        lhs = whalf[h % 2][:, 256 * a + 128 * fs_l:
                                           256 * a + 128 * fs_l + 128]
                        tk_mm = p.top(lambda t, lhs=lhs, a=a, pa=pa: t.matmul(
                            pa[:, :T], lhs, ZCH(a), start=(a == 0), stop=(a == 15)))
                    tk_ch[h] = tk_mm
                    evt[fc] = evac(fc, pa, tk_mm)
                if h + 2 < 8:
                    p.wait("sync", tk_ch[h])
                    wdma(h + 2)
            p.wait("tensor", evt[15])
            return evt[15]

        ev_toks = {}

        def mk_evac(buf, eng="vector"):
            def evac(fc, pa, tk_mm):
                dst = buf[:, 512 * fc:512 * fc + T]
                p.wait(eng, tk_mm)
                if eng == "vector":
                    ev_toks[fc] = p.vop(lambda v, dst=dst, pa=pa:
                                        v.tensor_copy(dst, pa[:, :T]))
                else:
                    ev_toks[fc] = p.sop(lambda sx, dst=dst, pa=pa:
                                        sx.copy(dst, pa[:, :T]))
                return ev_toks[fc]
            return evac

        vperm = v_alt.rearrange("p (u a g j) -> p u g a j", a=4, g=4, j=32)

        def evac_v(fc, pa, tk_mm):
            d_up = vperm[:, 0:T // 16, :, :, fc]
            d_lo = vperm[:, 0:T // 16, :, :, fc + 16]
            src = pa[0:64, :T].rearrange("p (u g a) -> p u g a", g=4, a=4)
            src2 = pa[64:128, :T].rearrange("p (u g a) -> p u g a", g=4, a=4)
            p.wait("vector", tk_mm)
            p.vop(lambda v, d_up=d_up, src=src: v.tensor_copy(d_up, src))
            ev_toks[fc] = p.vop(lambda v, d_lo=d_lo, src2=src2:
                                v.tensor_copy(d_lo, src2))
            return ev_toks[fc]

        def rope(buf, ev):
            p.waitl("tensor", [ev[NCH - 1], tk_sin, tk_cos, p.now("V"), p.now("S")])
            tk6 = None
            tkce = tkse = None
            for c in range(NCH):
                ch = buf[:, 512 * c:512 * c + T]
                lhsE = ecssb[:, 128 * c:128 * c + 128]
                if tk6 is not None:
                    p.waitl("tensor", [tk6, tkce, tkse])
                tkc = p.top(lambda t, lhsE=lhsE: t.matmul(
                    bcA[:, :T], lhsE, cossb[:], start=True, stop=True))
                tks = p.top(lambda t, lhsE=lhsE: t.matmul(
                    bcB[:, :T], lhsE, sinsb[:], start=True, stop=True))
                tkp = p.top(lambda t, ch=ch: t.matmul(
                    pscr[:, :T], pmsb[:], ch, start=True, stop=True))
                p.wait("scalar", tkc)
                tkce = p.sop(lambda s: s.copy(cexp[:, :T], bcA[:, :T]))
                p.wait("scalar", tks)
                tkse = p.sop(lambda s: s.copy(sexp[:, :T], bcB[:, :T]))
                p.wait("vector", tkp)
                p.vop(lambda v: v.tensor_copy(pq_sc[:, :T], pscr[:, :T]))
                p.wait("vector", tkce)
                p.vop(lambda v, ch=ch: v.tensor_mul(u2ps[:, :T], ch, cexp[:, :T]))
                p.wait("vector", tkse)
                p.vop(lambda v, ch=ch: v.tensor_mul(ch, pq_sc[:, :T], sexp[:, :T]))
                tk6 = p.vop(lambda v, ch=ch: v.tensor_add(ch, ch, u2ps[:, :T]))
            return tk6

        arE_prev = p.now("T")
        proj(wqb, mk_evac(q_buf), [], [], predma=q_pre, wf32=wq)
        q_ev = dict(ev_toks); ev_toks.clear()
        tk_qrope = rope(q_buf, q_ev)
        proj(wkb, mk_evac(k_buf), [], [], wf32=wk)
        k_ev = dict(ev_toks); ev_toks.clear()
        tk_krope = rope(k_buf, k_ev)
        proj(wvb, evac_v, [], [], wf32=wv)
        v_ev = dict(ev_toks); ev_toks.clear()
        snap(4)
        snap(5)

        fm_toks = []
        prev_scores_reader = None
        prev_sc = {}
        prev_PT_reader = None
        for half in range(NHALF):
            sb_done = []
            for Gl in range(GPH):
                G = half * GPH + Gl
                SG, MG = S_grp, M_grp
                G2 = 0
                tbase = 64 * G
                pack_toks = []
                ppr = prev_sc.get(0)
                p.waitl("vector", [tk_qrope, tk_krope, ppr, tk_ms2])
                p.waitl("gpsimd", [tk_qrope, tk_krope, ppr, tk_ms2])
                for slot in range(4):
                    scol = [0, 32, 64, 96][slot]
                    mcol = [0, 0, 32, 32][slot]
                    for ih in range(2):
                        srcq = q_buf[64 * ih:64 * ih + 64, :].rearrange(
                            "p (c g4 s) -> p c g4 s", g4=128, s=4)[
                            :, :, tbase // 4:tbase // 4 + 16, slot].transpose([0, 2, 1])
                        dstq = SG[64 * (slot % 2):64 * (slot % 2) + 64, :].rearrange(
                            "p (g q) -> p g q", q=128)[
                            :, :, scol + 16 * ih:scol + 16 * ih + 16]
                        pack_toks.append(p.vop(
                            lambda v, d=dstq, s=srcq: v.tensor_copy(d, s)))
                        srck = k_buf[64 * ih:64 * ih + 64, :].rearrange(
                            "p (c g4 s) -> p c g4 s", g4=128, s=4)[
                            :, :, tbase // 4:tbase // 4 + 16, slot].transpose([0, 2, 1])
                        dstk = MG[64 * (slot % 2):64 * (slot % 2) + 64, :].rearrange(
                            "p (g q) -> p g q", q=64)[
                            :, :, mcol + 16 * ih:mcol + 16 * ih + 16]
                        pack_toks.append(p.gop(
                            lambda g, d=dstk, s=srck: g.tensor_copy(d, s)))
                snap(61)
                p.waitl("tensor", pack_toks + [prev_scores_reader])
                for g in range(16):
                    p.top(lambda t, g=g: t.matmul(
                        pscr[0:64, 32 * g:32 * g + 32],
                        SG[:, 128 * g:128 * g + 64],
                        MG[:, 64 * g:64 * g + 32], start=True, stop=True))
                    tk_sc = p.top(lambda t, g=g: t.matmul(
                        pscr[64:128, 32 * g:32 * g + 32],
                        SG[:, 128 * g + 64:128 * g + 128],
                        MG[:, 64 * g + 32:64 * g + 64], start=True, stop=True))
                prev_sc[0] = tk_sc
                snap(62)
                p.wait("vector", tk_sc)
                p.waitl("vector", [prev_PT_reader])
                p.vop(lambda v: v.reduce_max(
                    smax[:], pscr[:].rearrange("p (g j) -> p g j", j=32), axis=AX.X))
                tk_sub = p.vop(lambda v: v.tensor_sub(
                    sc_sc[:].rearrange("p (g j) -> p g j", j=32),
                    pscr[:].rearrange("p (g j) -> p g j", j=32),
                    smax[:, :, None].broadcast_to([128, 16, 32])))
                prev_scores_reader = tk_sub
                p.wait("scalar", tk_sub)
                tk_exp = p.sop(lambda s: s.activation(sc_sc[:], sc_sc[:], AF.Exp,
                                                      scale=0.125))
                p.wait("vector", tk_exp)
                p.vop(lambda v: v.reduce_sum(
                    ssum[:], sc_sc[:].rearrange("p (g j) -> p g j", j=32), axis=AX.X))
                p.vop(lambda v: v.reciprocal(srec[:], ssum[:]))
                tk_pr = p.vop(lambda v: v.tensor_mul(
                    P_grp[:].rearrange("p (g j) -> p g j", j=32),
                    sc_sc[:].rearrange("p (g j) -> p g j", j=32),
                    srec[:, :, None].broadcast_to([128, 16, 32])))
                if it == 0:
                    pump(1)
                snap(63)
                if DBG:
                    p.wait("scalar", tk_pr)
                    tk_dp = p.dma("scalar", dbg_p[it, G][:], P_grp[:], tick="DXP")
                    p.waitl("vector", [tk_dp])
                tk_pve = None
                for u in range(4):
                    ug = 4 * Gl + u
                    p.waitl("tensor", [tk_pr])
                    tk_t1 = p.top(lambda t, u=u: t.transpose(
                        bfA[:, 128 * u:128 * u + 128],
                        P_grp[:, 128 * u:128 * u + 128], idbsb[:]))
                    prev_PT_reader = tk_t1
                    p.wait("vector", tk_t1)
                    for g_ in range(4):
                        src = bfA[32 * g_:32 * g_ + 32,
                                  128 * u:128 * u + 128].rearrange(
                            "p (a i) -> p a i", i=32)
                        dst = P4sb[32 * g_:32 * g_ + 32, :].rearrange(
                            "p (a w) -> p a w", w=128)[:, :, 32 * g_:32 * g_ + 32]
                        tk_pt = p.vop(lambda v, src=src, dst=dst:
                                      v.tensor_copy(dst, src))
                    snap(631)
                    if tk_pve is not None:
                        p.wait("tensor", tk_pve)
                    p.waitl("tensor", [v_ev[NCH - 1]])
                    for m_ in range(8):
                        pos0 = 16 * u + 2 * m_
                        src = v_alt[:, 32 * (64 * G + pos0):
                                    32 * (64 * G + pos0) + 64]
                        tk_vt = p.top(lambda t, m_=m_, src=src: t.transpose(
                            bfB[0:64, 64 * m_:64 * m_ + 64],
                            src, idbsb[0:64, 0:64]))
                    p.wait("vector", tk_vt)
                    for par in range(2):
                        vsrc = bfB[0:64, 0:512].rearrange(
                            "p (m c) -> p m c", c=64)[:, par::2, :]
                        vdst = V4sb[64 * par:64 * par + 64, :].rearrange(
                            "p (a c) -> p a c", c=64)
                        tk_vp = p.vop(lambda v, vsrc=vsrc, vdst=vdst:
                                      v.tensor_copy(vdst, vsrc))
                    snap(632)
                    p.waitl("tensor", [tk_pt, tk_vp])
                    for a_ in range(4):
                        tk_pv = p.top(lambda t, a_=a_: t.matmul(
                            accB[:, 64 * a_:64 * a_ + 64],
                            P4sb[:, 128 * a_:128 * a_ + 128],
                            V4sb[:, 64 * a_:64 * a_ + 64],
                            start=True, stop=True))
                    p.wait("vector", tk_pv)
                    tk_pve = p.vop(lambda v: v.tensor_copy(pv_sb[:], accB[:, 0:256]))
                    snap(633)
                    p.wait("tensor", tk_pve)
                    if it == 0:
                        pump(2)
                    for hh in range(2):
                        tk_dt = p.top(lambda t, hh=hh: t.transpose(
                            bfA[:, 512 + 128 * hh:512 + 128 * hh + 128],
                            pv_sb[:, 128 * hh:128 * hh + 128], idbsb[:]))
                    p.wait("vector", tk_dt)
                    sb_done.append(p.vop(lambda v, ug=ug: v.tensor_copy(
                        sbufT[:, 256 * ug:256 * ug + 256], bfA[:, 512:768])))
                    p.wait("tensor", sb_done[-1])
                    snap(64)
            lastT = p.now("T")
            p.waitl("vector", sb_done + [lastT])
            p.waitl("gpsimd", sb_done + [lastT])
            p.waitl("scalar", sb_done + [lastT])
            for ac in range(16):
                for hh2 in range(2):
                    i_ = ac + 16 * hh2
                    for qp in range(2):
                        for hh in range(2):
                            src = sbufT[64 * qp:64 * qp + 64, :].rearrange(
                                "p (ug hb s i) -> p ug hb s i", hb=2, s=4, i=32)[
                                :, 0:UGH, hh, :, i_]
                            dst = attnfm[64 * hh2:64 * hh2 + 64, :].rearrange(
                                "p (c t) -> p c t", t=512)[:, :, 0:T].rearrange(
                                "p c (hb ug q s) -> p c hb ug q s",
                                hb=NHALF, ug=UGH, q=4, s=4)[
                                :, ac, half, :, :, 2 * hh + qp]
                            k3 = (2 * ac + hh2) % 3
                            if k3 == 0:
                                fm_toks.append(p.vop(
                                    lambda e, s=src, d=dst: e.tensor_copy(d, s)))
                            elif k3 == 1:
                                fm_toks.append(p.sop(
                                    lambda e, s=src, d=dst: e.copy(d, s)))
                            else:
                                fm_toks.append(p.gop(
                                    lambda e, s=src, d=dst: e.tensor_copy(d, s)))
            snap(65)

        snap(6)
        if DBG:
            p.waitl("scalar", [tk_qrope, tk_krope, v_ev[NCH - 1]] + fm_toks)
            p.dma("scalar", dbg_q[it][:],
                  q_buf.rearrange("p (c t) -> p c t", t=512)[:, :, 0:T], tick="DXP")
            p.dma("scalar", dbg_k[it][:],
                  k_buf.rearrange("p (c t) -> p c t", t=512)[:, :, 0:T], tick="DXP")
            p.dma("scalar", dbg_v[it][:],
                  v_alt.rearrange("p (j t) -> p j t", t=512)[:, :, 0:T], tick="DXP")
            tk_dbg = p.dma("scalar", dbg_fm[it][:],
                           attnfm.rearrange("p (c t) -> p c t", t=512)[:, :, 0:T],
                           tick="DXP")
            p.waitl("vector", [tk_dbg])
            p.waitl("gpsimd", [tk_dbg])
            p.waitl("tensor", [tk_dbg])

        p.waitl("tensor", fm_toks)
        p.waitl("sync", [p.now("T")])
        tk_xpw = {}
        tk_wo = {}
        tk_evo = {}
        tk_cho = {}

        def wodma(h):
            if it == 0:
                tk_wo[h] = qkvo_fetch(wob, wo, h, [tk_cho.get(h - 2)])
                pump(2)
                return
            tk_wo[h] = p.dma("sync", whalf[h % 2][:], wob[h][:],
                             tick="DW" if h % 2 == 0 else "DG")

        wodma(0)
        wodma(1)
        for h in range(8):
            fC, pair = h // 2, h % 2
            p.wait("tensor", tk_wo[h])
            for fs_l in range(2):
                dc = 4 * fC + 2 * pair + fs_l
                pa = (accA, accB, bcA, bcB)[dc % 4]
                if dc - 2 in tk_evo:
                    p.wait("scalar", tk_evo[dc - 2])
                tk_xd = p.dma("scalar", xbuf(dc), xsl[:, dc, :],
                              tick="DX0" if dc % 2 == 0 else "DX1")
                if dc - 4 in tk_evo:
                    p.wait("tensor", tk_evo[dc - 4])
                for a in range(16):
                    lhs = whalf[h % 2][:, 256 * a + 128 * fs_l:
                                       256 * a + 128 * fs_l + 128]
                    rhs = attnfm[:, 512 * a:512 * a + T]
                    tk_mm = p.top(lambda t, lhs=lhs, rhs=rhs, a=a, pa=pa: t.matmul(
                        pa[:, :T], lhs, rhs, start=(a == 0), stop=(a == 15)))
                tk_cho[h] = tk_mm
                p.wait("vector", tk_mm)
                p.wait("vector", tk_xd)
                if dc >= 2:
                    p.wait("vector", tk_xpw[dc - 2])
                tk_evo[dc] = p.vop(lambda v, dc=dc, pa=pa: v.tensor_add(
                    STG(dc), pa[:, :T], xbuf(dc)))
                p.wait("scalar", tk_evo[dc])
                tk_xpw[dc] = p.dma("scalar", xpsl[:, dc, :], STG(dc),
                                   tick="DO0" if dc % 2 == 0 else "DO1")
            if h + 2 < 8:
                p.wait("sync", tk_cho[h])
                wodma(h + 2)
        tk_womm = p.now("T")
        tk_xp_all = tk_xpw[NCH - 1]
        snap(7)

        p.waitl("vector", [tk_womm])
        p.waitl("gpsimd", [tk_womm])
        p.waitl("scalar", [p.now("DO0"), p.now("DO1")])
        tk_dma3 = {}
        for c in range(min(2, NCH)):
            tk_dma3[c] = p.dma("scalar", XCC(c), xpsl[:, c, :], tick="DX0" if c % 2 == 0 else "DX1")
        for c in range(NCH):
            if it == 0:
                pump(1)
            p.wait("tensor", tk_dma3[c])
            tk_m3 = p.top(lambda t, c=c: t.matmul(
                pscr[0:1, :T], onrsb[:], XCC(c), start=(c == 0), stop=(c == NCH - 1)))
            if c + 2 < NCH:
                p.wait("scalar", tk_m3)
                tk_dma3[c + 2] = p.dma("scalar", XCC(c + 2), xpsl[:, c + 2, :],
                                       tick="DX0" if c % 2 == 0 else "DX1")
        p.wait("vector", tk_m3)
        tk_m3r = p.vop(lambda v: v.tensor_scalar(ROWS(R_M1), pscr[0:1, :T],
                                                 1.0 / D, None, ALU.mult))
        p.wait("tensor", tk_m3r)
        tk_m3b = rank1(bcA[:, :T], R_M1)
        p.wait("vector", tk_m3b)
        tk_s23 = {}
        for c in range(NCH):
            sq_c = sqs[:, 512 * (c % 2):512 * (c % 2) + T]
            if c >= 2:
                p.wait("vector", tk_s23[c - 2])
            p.vop(lambda v, c=c: v.tensor_sub(XCC(c), XCC(c), bcA[:, :T]))
            tk_sq3 = p.vop(lambda v, c=c, sq_c=sq_c: v.tensor_mul(sq_c, XCC(c), XCC(c)))
            p.wait("tensor", tk_sq3)
            tk_s23[c] = p.top(lambda t, c=c, sq_c=sq_c: t.matmul(
                pscr[0:1, :T], onrsb[:], sq_c, start=(c == 0), stop=(c == NCH - 1)))
        p.wait("scalar", tk_s23[NCH - 1])
        p.sop(lambda s: s.activation(ROWS(R_T1), pscr[0:1, :T], AF.Ln,
                                     bias=epsc[RBASE(R_T1):RBASE(R_T1) + 1],
                                     scale=1.0 / D))
        tk_r3 = p.sop(lambda s: s.activation(ROWS(R_R1), ROWS(R_T1), AF.Exp,
                                             scale=-0.5))
        p.wait("tensor", tk_r3)
        tk_r3b = rank1(bcB[:, :T], R_R1)
        prev_r3b = tk_r3b
        p.wait("vector", tk_r3b)
        tk_xn = {}
        for c in range(NCH):
            tk_xn[c] = p.vop(lambda v, c=c: v.tensor_mul(XN(c), XCC(c), bcB[:, :T]))
        tk_xn_all = tk_xn[NCH - 1]
        snap(8)

        p.waitl("tensor", [tk_xn_all])
        for a in range(16):
            tk_lg = p.top(lambda t, a=a: t.matmul(
                pscr[0:NE, :T], wrsb[:, 8 * a:8 * a + 8], XN(a),
                start=(a == 0), stop=(a == 15)))
        p.wait("scalar", tk_lg)
        tk_lgs = p.sop(lambda s: s.activation(
            lgsb[:, 0:T], pscr[0:NE, :T], AF.Identity, bias=brsb[32:32 + NE]))
        if it == 0:
            pump(8)
        p.wait("tensor", tk_lgs)
        for sub in range(NSUB):
            tk_lt = p.top(lambda t, sub=sub: t.transpose(
                pscr[0:TS_, 8 * sub:8 * sub + 8],
                lgsb[:, TS_ * sub:TS_ * sub + TS_],
                idfsb[32:32 + NE, 32:32 + NE], tile_position=(32, 0)))
        p.wait("vector", tk_lt)
        p.vop(lambda v: v.tensor_copy(lgtm[0:TS_, :], pscr[0:TS_, 0:8 * NSUB]))
        for sub in range(NSUB):
            L = lgtm[0:TS_, 8 * sub:8 * sub + 8]
            p.vop(lambda v, L=L: v.max(srt8[0:TS_, :], L))
            p.vop(lambda v, L=L: v.max_index(idx8[0:TS_, :], srt8[0:TS_, :], L))
            p.vop(lambda v: v.tensor_copy(idxf[0:TS_, :], idx8[0:TS_, :]))
            tk_ng = p.vop(lambda v: v.tensor_scalar(
                negm[0:TS_, 0:1], srt8[0:TS_, 0:1], -1.0, None, ALU.mult))
            p.wait("scalar", tk_ng)
            tk_ex = p.sop(lambda s, L=L: s.activation(esc8[0:TS_, :], L, AF.Exp,
                                                      bias=negm[0:TS_, 0:1]))
            p.wait("vector", tk_ex)
            p.vop(lambda v: v.reduce_sum(negm[0:TS_, 1:2], esc8[0:TS_, :], axis=AX.X))
            p.vop(lambda v: v.reciprocal(negm[0:TS_, 1:2], negm[0:TS_, 1:2]))
            p.vop(lambda v, sub=sub: v.tensor_copy(
                mets[0:TS_, 4 * sub:4 * sub + 1], idxf[0:TS_, 0:1]))
            p.vop(lambda v, sub=sub: v.tensor_copy(
                mets[0:TS_, 4 * sub + 1:4 * sub + 2], idxf[0:TS_, 1:2]))
            p.vop(lambda v, sub=sub: v.tensor_copy(
                mets[0:TS_, 4 * sub + 2:4 * sub + 3], negm[0:TS_, 1:2]))
            tk_d21 = p.vop(lambda v: v.tensor_sub(
                negm[0:TS_, 2:3], srt8[0:TS_, 1:2], srt8[0:TS_, 0:1]))
            p.wait("scalar", tk_d21)
            tk_e2 = p.sop(lambda s: s.activation(
                negm[0:TS_, 3:4], negm[0:TS_, 2:3], AF.Exp))
            p.wait("vector", tk_e2)
            tk_msub = p.vop(lambda v, sub=sub: v.tensor_mul(
                mets[0:TS_, 4 * sub + 3:4 * sub + 4], negm[0:TS_, 3:4],
                negm[0:TS_, 1:2]))
        p.wait("tensor", tk_msub)
        for sub in range(NSUB):
            tk_mt = {}
            for m in range(4):
                tk_mt[m] = p.top(lambda t, sub=sub, m=m: t.transpose(
                    pscr[0:1, TS_ * m:TS_ * m + TS_],
                    mets[0:TS_, 4 * sub + m:4 * sub + m + 1],
                    idfsb[0:TS_, 0:TS_]))
            p.wait("vector", tk_mt[3])
            for m, ri in ((0, R_I0), (1, R_I1), (2, R_W0), (3, R_W1)):
                tk_tr = p.vop(lambda v, sub=sub, m=m, ri=ri: v.tensor_copy(
                    ROWS(ri)[:, TS_ * sub:TS_ * sub + TS_],
                    pscr[0:1, TS_ * m:TS_ * m + TS_]))
            p.wait("tensor", tk_tr)
        tk_ws0 = p.vop(lambda v: v.tensor_add(ROWS(R_WS), ROWS(R_W0), ROWS(R_W1)))
        tk_ws = p.vop(lambda v: v.tensor_copy(wsrow64[:, 0:T], ROWS(R_WS)))
        p.waitl("tensor", [tk_ws])
        tk_i0b = p.top(lambda t: t.matmul(
            bcA[0:64, :T], onesm[RBASE(R_I0):RBASE(R_I0) + 1, 0:64], ROWS(R_I0),
            start=True, stop=True, tile_position=(RBASE(R_I0), 0)))
        tk_i1b = p.top(lambda t: t.matmul(
            bcB[0:64, :T], onesm[RBASE(R_I1):RBASE(R_I1) + 1, 0:64], ROWS(R_I1),
            start=True, stop=True, tile_position=(RBASE(R_I1), 0)))
        p.wait("vector", tk_i0b)
        tk_oh0 = p.vop(lambda v: v.tensor_scalar(ohm0[:, :T], bcA[0:64, :T], eisb[:],
                                                 None, ALU.is_equal))
        p.wait("vector", tk_i1b)
        tk_oh1 = p.vop(lambda v: v.tensor_scalar(ohm1[:, :T], bcB[0:64, :T], eisb[:],
                                                 None, ALU.is_equal))
        p.waitl("tensor", [tk_oh0, tk_oh1])
        tk_w0b = rank1(bcA[:, :T], R_W0)
        tk_w1b = rank1(bcB[:, :T], R_W1)
        p.wait("vector", tk_w0b)
        tk_wb0 = p.vop(lambda v: v.tensor_copy(wb0[:, :T], bcA[:, :T]))
        p.wait("vector", tk_w1b)
        tk_wb1 = p.vop(lambda v: v.tensor_copy(wb1[:, :T], bcB[:, :T]))

        for a in range(16):
            tk_u1 = p.top(lambda t, a=a: t.matmul(
                u2ps[0:ER, :T], a1sb[:, 64 * a:64 * a + 64], XN(a),
                start=(a == 0), stop=(a == 15)))
        p.wait("vector", tk_u1)
        tk_u1e = p.vop(lambda v: v.tensor_copy(u1sb[:, :T], u2ps[0:ER, :T]))
        tk_z10 = p.vop(lambda v: v.scalar_tensor_tensor(
            z1k0[:, :T], u1sb[:, :T], 2.0, ohm0[:, :T], ALU.mult, ALU.mult))
        tk_z11 = p.vop(lambda v: v.scalar_tensor_tensor(
            z1k1[:, :T], u1sb[:, :T], 2.0, ohm1[:, :T], ALU.mult, ALU.mult))
        snap(9)

        p.waitl("tensor", [tk_z10, tk_z11, tk_wb0, tk_wb1, tk_u1e, p.now("G")])
        p.waitl("vector", [p.now("G")])
        if it == 0:
            pump_class("w1")
            pump_class("b1")
        p.waitl("sync", [tk_z11, tk_womm, p.now("T"), p.now("V"), p.now("G")])
        if it == 0:
            p.waitl("sync", feed_done["w1"] + feed_done["b1"])
        w1bufs = [w1buf, bc16(arB[:, 0:4096])]
        tk_b1d = {}
        tk_hw = {}
        tk_w1d = {}
        tk_chain = {}
        tk_b1e = {}
        tk_h0 = {}
        tk_h1 = {}
        tk_rd1 = {}
        tk_b1h0 = {}
        tk_u2t = {}

        def w1_dma(fCn):
            tk_w1d[fCn] = p.dma(
                "sync", w1bufs[fCn % 2][:], w1b[fCn][:],
                tick="DW" if fCn % 2 == 0 else "DG")

        tk_b1d[0] = p.dma("sync", b1f0[:], b1rfb[:, 0:512], tick="DB0")
        tk_b1d[1] = p.dma("sync", b1f1[:], b1rfb[:, 512:1024], tick="DB1")
        w1_dma(0)
        w1_dma(1)

        def w1_chain(fc):
            fC, fs = fc // 4, fc % 4
            if fs == 0:
                if it == 0 and fC < 8:
                    pump(4)
                p.wait("tensor", tk_w1d[fC])
            if fc - 2 in tk_b1e:
                p.wait("tensor", tk_b1e[fc - 2])
            pa = accA if fc % 2 == 0 else accB
            wb = w1bufs[fC % 2]
            for a in range(16):
                lhs = wb[:, 512 * a + 128 * fs:512 * a + 128 * fs + 128]
                tk = p.top(lambda t, lhs=lhs, a=a, pa=pa: t.matmul(
                    pa[:, :T], lhs, XN(a), start=(a == 0), stop=(a == 15)))
            tk_chain[fc] = tk
            if fs == 3 and fC + 2 < 16:
                p.wait("sync", tk_chain[fc])
                w1_dma(fC + 2)

        def emit_b1e(fc):
            p.wait("scalar", tk_chain[fc])
            if fc >= 1:
                p.wait("scalar", tk_rd1[fc - 1])
            pa = accA if fc % 2 == 0 else accB
            tk_b1e[fc] = p.sop(lambda s, fc=fc, pa=pa: s.activation(
                base1[:, :T], pa[:, :T], AF.Identity, bias=b1csb[:, fc:fc + 1]))

        def emit_gelus(fc):
            if fc >= 1:
                p.wait("scalar", tk_u2t[fc - 1])
                p.wait("scalar", tk_hw[fc - 1])
            p.wait("scalar", tk_b1h0[fc])
            tk_h0[fc] = p.sop(lambda s: s.activation(
                h0sb[:, :T], bcA[:, :T], AF.Gelu_apprx_tanh))
            p.wait("scalar", tk_rd1[fc])
            tk_h1[fc] = p.sop(lambda s: s.activation(
                h1sb[:, :T], bcB[:, :T], AF.Gelu_apprx_tanh))

        def emit_readds(fc):
            fC, fs = fc // 4, fc % 4
            b1buf = b1f0 if fC % 2 == 0 else b1f1
            p.wait("tensor", tk_b1e[fc])
            p.wait("tensor", tk_b1d[fC])
            if fc >= 1:
                p.wait("tensor", tk_h0[fc - 1])
            p.top(lambda t: t.matmul(
                bcA[:, :T], idbsb[:], base1[:, :T], start=True, stop=False))
            tk_b1h0[fc] = p.top(lambda t, fs=fs, b1buf=b1buf: t.matmul(
                bcA[:, :T], b1buf[:, 128 * fs:128 * fs + 128], z1k0[:, :T],
                start=False, stop=True))
            if fc >= 1:
                p.wait("tensor", tk_h1[fc - 1])
            p.top(lambda t: t.matmul(
                bcB[:, :T], idbsb[:], base1[:, :T], start=True, stop=False))
            tk_rd1[fc] = p.top(lambda t, fs=fs, b1buf=b1buf: t.matmul(
                bcB[:, :T], b1buf[:, 128 * fs:128 * fs + 128], z1k1[:, :T],
                start=False, stop=True))
            if fs == 3 and fC + 2 < 16:
                b1buf_n = b1f0 if (fC + 2) % 2 == 0 else b1f1
                p.wait("sync", tk_rd1[fc])
                tk_b1d[fC + 2] = p.dma(
                    "sync", b1buf_n[:],
                    b1rfb[:, 512 * (fC + 2):512 * (fC + 2) + 512],
                    tick="DB0" if (fC + 2) % 2 == 0 else "DB1")

        def emit_u2(fc):
            p.waitl("tensor", [tk_h0[fc], tk_h1[fc]])
            p.top(lambda t, fc=fc: t.matmul(
                u2ps[0:ER, :T], a2sb[:, 64 * fc:64 * fc + 64], h0sb[:, :T],
                start=(fc == 0), stop=(fc == NFC - 1)))
            tk_u2t[fc] = p.top(lambda t, fc=fc: t.matmul(
                pscr[0:ER, :T], a2sb[:, 64 * fc:64 * fc + 64], h1sb[:, :T],
                start=(fc == 0), stop=(fc == NFC - 1)))

        def emit_hw(fc):
            hw_fc = hw[:, T * fc:T * fc + T]
            if it == 0 and fc == 32:
                p.waitl("vector", [feed["ce"][0], feed["ce"][1]])
                p.waitl("vector", [feed["pb"][0], feed["pb"][1]])
            p.waitl("vector", [tk_h0[fc], tk_h1[fc]])
            p.vop(lambda v: v.tensor_mul(tsc[:, :T], h0sb[:, :T], wb0[:, :T]))
            p.vop(lambda v, hw_fc=hw_fc: v.tensor_mul(
                hw_fc, h1sb[:, :T], wb1[:, :T]))
            tk_hw[fc] = p.vop(lambda v, hw_fc=hw_fc: v.tensor_add(
                hw_fc, hw_fc, tsc[:, :T]))

        w1_chain(0)
        for fc in range(NFC):
            if fc + 1 < NFC:
                w1_chain(fc + 1)
            emit_b1e(fc)
            if fc >= 1:
                emit_u2(fc - 1)
                emit_hw(fc - 1)
            emit_readds(fc)
            emit_gelus(fc)
        emit_u2(NFC - 1)
        emit_hw(NFC - 1)
        tk_u2 = tk_u2t[NFC - 1]
        tk_hw_all = tk_hw[NFC - 1]
        snap(10)

        p.wait("vector", tk_u2)
        zs0 = zsc0[:, 0:T]
        tk_wk0 = p.top(lambda t: t.matmul(
            bcA[0:64, :T], onesm[RBASE(R_W0):RBASE(R_W0) + 1, 0:64], ROWS(R_W0),
            start=True, stop=True, tile_position=(RBASE(R_W0), 0)))
        tk_wk1 = p.top(lambda t: t.matmul(
            bcB[0:64, :T], onesm[RBASE(R_W1):RBASE(R_W1) + 1, 0:64], ROWS(R_W1),
            start=True, stop=True, tile_position=(RBASE(R_W1), 0)))
        p.vop(lambda v: v.scalar_tensor_tensor(
            zs0, u2ps[0:ER, :T], 2.0, ohm0[:, :T], ALU.mult, ALU.mult))
        p.wait("vector", tk_wk0)
        p.vop(lambda v: v.tensor_mul(zs0, zs0, bcA[0:64, :T]))
        p.vop(lambda v: v.scalar_tensor_tensor(
            z2wsb[:, :T], pscr[0:ER, :T], 2.0, ohm1[:, :T], ALU.mult, ALU.mult))
        p.wait("vector", tk_wk1)
        p.vop(lambda v: v.tensor_mul(z2wsb[:, :T], z2wsb[:, :T], bcB[0:64, :T]))
        tk_z2w = p.vop(lambda v: v.tensor_add(z2wsb[:, :T], z2wsb[:, :T], zs0))
        snap(111)

        if it == 0:
            if T == 512:
                assert not feed_pend, f"feeder leftover {len(feed_pend)}"
            pump_class("w2")
        p.waitl("tensor", [tk_z2w, tk_hw_all, tk_ws])
        p.waitl("sync", [tk_xn_all, p.now("T"), p.now("V")])
        if it == 0:
            p.waitl("sync", feed_done["w2"])
        out_toks = {}
        evd = {}
        last_w2_reader = None
        for dC in range(8):
            if last_w2_reader is not None:
                p.wait("sync", last_w2_reader)
            tk_w = p.dma("sync", w2buf[:], w2b[dC][:], tick="DW")
            p.wait("tensor", tk_w)
            snap(112)
            for dh in range(2):
                dc = 2 * dC + dh
                pa = (accA, accB, bcA, bcB)[dc % 4]
                if dc - 4 in evd:
                    p.wait("tensor", evd[dc - 4])
                snap(113)
                p.top(lambda t, dc=dc, pa=pa: t.matmul(
                    pa[:, :T], b2sb[0:65, 128 * dc:128 * dc + 128],
                    bc16(arD[0:65, 2304:2560])[:, :T],
                    start=True, stop=False))
                snap(114)
                for fc in range(NFC):
                    lhs = w2buf[:, 256 * fc + 128 * dh:256 * fc + 128 * dh + 128]
                    tk_mm = p.top(lambda t, lhs=lhs, fc=fc, pa=pa: t.matmul(
                        pa[:, :T], lhs, hw[:, T * fc:T * fc + T],
                        start=False, stop=(fc == NFC - 1)))
                last_w2_reader = tk_mm
                snap(115)
                if dc - 2 in evd:
                    p.wait("scalar", evd[dc - 2])
                tk_xd = p.dma("scalar", xbuf(dc), xpsl[:, dc, :], tick="DX0" if dc % 2 == 0 else "DX1")
                p.wait("vector", tk_mm)
                p.wait("vector", tk_xd)
                if dc >= 2:
                    p.wait("vector", out_toks[dc - 2])
                evd[dc] = p.vop(lambda v, dc=dc, pa=pa: v.tensor_add(
                    STG(dc), pa[:, :T], xbuf(dc)))
                p.wait("scalar", evd[dc])
                out_toks[dc] = p.dma("scalar", outsl[:, dc, :], STG(dc), tick="DO0" if dc % 2 == 0 else "DO1")
                snap(116)

    if snaps:
        qlen, cnts = snaps[0]
        for e in list(p.q):
            p.q[e] = p.q[e][:qlen[e]]
        p.cnt = cnts
        p.join_all()
        tk_ms = p.vop(lambda v: v.memset(arX[:, 0:512], 0.0))
        p.wait("scalar", tk_ms)
        for it in range(NT):
            outsl_ = out.rearrange("(c p) t -> p c t", p=128)[:, :, it * T:it * T + T]
            for c in range(NCH):
                p.dma("scalar", outsl_[:, c, :], arX[:, 0:T],
                      tick="DO0" if c % 2 == 0 else "DO1")
    for sname in ("DO0", "DO1"):
        if p.cnt[sname]:
            p.wait("sync", (sname, p.cnt[sname]))
    p.play()
    nc._keepalive = stx
    return nc


def prep_core_inputs(inputs, N_core, core_id, T):
    f32 = np.float32
    x = np.asarray(inputs["L0_attn_hidden_states"])
    x = x.reshape(-1, x.shape[-1])
    rows = slice(core_id * N_core, (core_id + 1) * N_core)
    d = {}
    d["xT"] = np.ascontiguousarray(np.asarray(x[rows]).T.astype(f32))

    def perm_cols(w):
        return np.ascontiguousarray(
            np.asarray(w).reshape(D, 32, 64)[:, HEAD_ORDER, :].reshape(D, D))

    def tile4(w):
        t4 = w.reshape(16, 128, 4, 512).transpose(2, 0, 1, 3)
        return np.ascontiguousarray(
            t4.reshape(4, 16, 128, 2, 256).transpose(0, 3, 2, 1, 4).reshape(
                8, 128, 4096))

    d["wq"] = tile4(perm_cols(inputs["L0_attn_w_q"]))
    d["wk"] = tile4(perm_cols(inputs["L0_attn_w_k"]))
    d["wv"] = tile4(perm_cols(inputs["L0_attn_w_v"]))
    wo_p = np.ascontiguousarray(
        np.asarray(inputs["L0_attn_w_o"]).reshape(32, 64, D)[HEAD_ORDER].reshape(D, D))
    d["wo"] = tile4(wo_p)
    w1m = np.asarray(inputs["L0_ffn_W1"])
    d["w1"] = np.ascontiguousarray(
        np.asarray(w1m).T.reshape(16, 128, 16, 512).transpose(2, 0, 1, 3)
        .transpose(0, 2, 1, 3).reshape(16, 128, 8192))
    w2m = np.asarray(inputs["L0_ffn_W2"])
    d["w2"] = np.ascontiguousarray(
        np.asarray(w2m).T.reshape(64, 128, 8, 256).transpose(2, 0, 1, 3)
        .transpose(0, 2, 1, 3).reshape(8, 128, 16384))
    a1 = np.asarray(inputs["L0_ffn_A1"]).reshape(ER, D)
    d["a1t"] = np.ascontiguousarray(np.asarray(a1).T.reshape(16, 128, ER))
    d["b1rf"] = np.ascontiguousarray(
        np.asarray(inputs["L0_ffn_B1"]).transpose(0, 2, 1).reshape(ER, FF))
    a2 = np.asarray(inputs["L0_ffn_A2"]).reshape(ER, FF)
    d["a2t"] = np.ascontiguousarray(np.asarray(a2).T.reshape(64, 128, ER))
    d["b2rf"] = np.ascontiguousarray(
        np.asarray(inputs["L0_ffn_B2"]).transpose(0, 2, 1).reshape(ER, D))
    wr = np.asarray(inputs["L0_ffn_W_r"])
    d["wrt"] = np.ascontiguousarray(np.asarray(wr).T.reshape(16, 128, NE))
    d["b1c"] = np.ascontiguousarray(
        np.asarray(inputs["L0_ffn_b1"]).reshape(NFC, 128).T.astype(f32))
    b2 = np.asarray(inputs["L0_ffn_b2"]).astype(f32)
    d["b2c4d"] = np.ascontiguousarray(b2.reshape(1, 2048))
    d["brrow"] = np.ascontiguousarray(
        np.asarray(inputs["L0_ffn_b_r"]).reshape(NE, 1).astype(f32))
    d["gam"] = np.ascontiguousarray(
        np.asarray(inputs["L0_attn_ln_gamma"]).reshape(NCH, 128).T.astype(f32))
    d["bet"] = np.ascontiguousarray(
        np.asarray(inputs["L0_attn_ln_beta"]).reshape(NCH, 128).T.astype(f32))
    d["identb"] = np.eye(128, dtype=ml_dtypes.bfloat16)
    d["identf"] = np.eye(128, dtype=f32)
    pm = np.zeros((64, 64), dtype=f32)
    for dd in range(32):
        pm[2 * dd + 1, dd] = -1.0
    for dd in range(32, 64):
        pm[2 * (dd - 32), dd] = 1.0
    pmat = np.zeros((128, 128), dtype=f32)
    pmat[0:64, 0:64] = pm
    pmat[64:128, 64:128] = pm
    d["pmat"] = pmat.astype(ml_dtypes.bfloat16)
    ecs = np.zeros((32, 2048), dtype=f32)
    for ac in range(16):
        ecs[ac, 128 * ac:128 * ac + 64] = 1.0
        ecs[ac + 16, 128 * ac + 64:128 * ac + 128] = 1.0
    d["ecs"] = ecs.astype(ml_dtypes.bfloat16)
    d["onesmd"] = np.ones((128, 128), dtype=f32)
    d["onesred"] = np.ones((128, 1), dtype=f32)
    d["eiota"] = (np.arange(64) // 8).astype(f32).reshape(64, 1)
    d["invf32"] = (1.0 / 10000.0 ** (np.arange(0, 64, 2).astype(f32) / f32(64.0))
                   ).astype(f32).reshape(32, 1)
    d["toff"] = np.full((32, 1), core_id * N_core, dtype=f32)
    d["epscol"] = np.full((128, 1), 1e-5, dtype=f32)
    d["iota32d"] = np.tile(np.arange(512, dtype=f32), (32, 1))
    return d


N_CORES = 8
N_TOK_TOTAL = 16384
N_CORE = N_TOK_TOTAL // N_CORES
TILE = 512

_NC_CACHE = {}


def _get_nc():
    if "nc" not in _NC_CACHE:
        nc = build(N_CORE, TILE)
        nc.finalize()
        _NC_CACHE["nc"] = nc
    return _NC_CACHE["nc"]


def kernel(**inputs):
    from concourse.bass_utils import run_bass_kernel_spmd
    nc = _get_nc()
    shared = prep_core_inputs(inputs, N_CORE, 0, TILE)
    xfull = np.asarray(inputs["L0_attn_hidden_states"])
    xfull = np.asarray(xfull).reshape(-1, D)
    in_maps = []
    for c in range(N_CORES):
        d = dict(shared)
        rows = slice(c * N_CORE, (c + 1) * N_CORE)
        d["xT"] = np.ascontiguousarray(np.asarray(xfull[rows]).T.astype(np.float32))
        d["toff"] = np.full((32, 1), c * N_CORE, dtype=np.float32)
        in_maps.append(d)
    res = run_bass_kernel_spmd(nc, in_maps, core_ids=list(range(N_CORES)))
    out = np.empty((N_TOK_TOTAL, D), dtype=np.float32)
    for c in range(N_CORES):
        out[c * N_CORE:(c + 1) * N_CORE] = np.asarray(res.results[c]["out"]).T
    return out.reshape(1, N_TOK_TOTAL, D)
